# version 10
# baseline (speedup 1.0000x reference)
import numpy as np
import ml_dtypes
import concourse.bass as bass
import concourse.mybir as mybir
from concourse.bass_utils import run_bass_kernel_spmd
from contextlib import ExitStack

F32 = mybir.dt.float32
BF16 = mybir.dt.bfloat16
I32 = mybir.dt.int32
AF = mybir.ActivationFunctionType
ALU = mybir.AluOpType
AX = mybir.AxisListType
BF = ml_dtypes.bfloat16

NCORE = 8
D = 2048
TCORE = 2048

_ENG_ATTR = {"sp": "sync", "pe": "tensor", "act": "scalar", "dve": "vector", "pool": "gpsimd"}
_STREAMS = ("pe", "act", "dve", "pool", "sp", "spd", "actd", "poold", "cc")


class Buf:
    __slots__ = ("name", "writers", "readers")

    def __init__(self, name):
        self.name = name
        self.writers = []
        self.readers = []


class Rec:
    __slots__ = ("eng", "stream", "fn", "inc", "sidx", "need_inc", "waits", "val", "tag")

    def __init__(self, eng, stream, fn, inc, sidx):
        self.eng = eng
        self.stream = stream
        self.fn = fn
        self.inc = inc
        self.sidx = sidx
        self.need_inc = False
        self.waits = ()
        self.val = 0
        self.tag = None


class T:
    __slots__ = ("h", "b")

    def __init__(self, h, name):
        self.h = h
        self.b = Buf(name)


class DT:
    __slots__ = ("ap", "b")

    def __init__(self, ap, name):
        self.ap = ap
        self.b = Buf(name)


_QCFG = {"spd": (16, 16), "actd": (8, 16), "poold": (16, 16), "cc": (4, 1)}
_ECHO_LAG = 14
_ECHO_BATCH = 8


class Ctx:
    def __init__(self, nc):
        self.nc = nc
        self.per_eng = {e: [] for e in _ENG_ATTR}
        self.per_stream = {s: [] for s in _STREAMS}
        self.maxwait = {}
        self.tag = None

    def op(self, eng, fn, reads=(), writes=(), dma=False, cc=False):
        stream = "cc" if cc else (eng + "d" if dma else eng)
        is_async = stream in _QCFG
        inc = _QCFG[stream][1] if is_async else 1
        rec = Rec(eng, stream, fn, inc, len(self.per_stream[stream]))
        if is_async:
            rec.need_inc = True
        pend = {}

        def want(p):
            if (not is_async) and p.stream == eng == "pe":
                return
            if self.maxwait.get((eng, p.stream), -1) >= p.sidx:
                return
            if pend.get(p.stream, (-1, None))[0] < p.sidx:
                pend[p.stream] = (p.sidx, p)

        for b in reads:
            for p in b.writers:
                want(p)
        for b in writes:
            for p in b.writers:
                want(p)
            for p in b.readers:
                want(p)
        waits = []
        for s, (sidx, p) in pend.items():
            self.maxwait[(eng, s)] = sidx
            p.need_inc = True
            waits.append(p)
        rec.waits = tuple(waits)
        rec.tag = self.tag
        self.per_eng[eng].append(rec)
        self.per_stream[stream].append(rec)
        for b in reads:
            b.readers.append(rec)
        for b in writes:
            b.writers = [rec]
            b.readers = []
        return rec

    def _barrier_waits(self, eng, exclude=()):
        waits = []
        for s in _STREAMS:
            if s in exclude or not self.per_stream[s]:
                continue
            p = self.per_stream[s][-1]
            if s not in _QCFG and s == eng:
                continue
            if self.maxwait.get((eng, s), -1) >= p.sidx:
                continue
            self.maxwait[(eng, s)] = p.sidx
            p.need_inc = True
            waits.append(p)
        return waits

    def phase(self, exclude=()):
        for eng in _ENG_ATTR:
            waits = self._barrier_waits(eng, exclude)
            if waits:
                rec = Rec(eng, eng, lambda e: e.nop(), 1, len(self.per_stream[eng]))
                rec.waits = tuple(waits)
                self.per_eng[eng].append(rec)
                self.per_stream[eng].append(rec)

    def finish(self):
        waits = self._barrier_waits("pool")
        rec = Rec("pool", "pool", lambda e: e.nop(), 1, len(self.per_stream["pool"]))
        rec.waits = tuple(waits)
        self.per_eng["pool"].append(rec)
        self.per_stream["pool"].append(rec)

    def build(self, es):
        nc = self.nc
        csem = {s: es.enter_context(nc.semaphore("sem_" + s))
                for s in _STREAMS if s not in _QCFG}
        raw = {}
        ords = {}
        for q, (k, _inc) in _QCFG.items():
            if not self.per_stream[q]:
                continue
            raw[q] = [es.enter_context(nc.semaphore(f"raw_{q}{i}")) for i in range(k)]
            ords[q] = es.enter_context(nc.semaphore(f"ord_{q}"))
        for s, recs in self.per_stream.items():
            if s in _QCFG:
                k, inc = _QCFG[s]
                for i, r in enumerate(recs):
                    r.val = inc * (i // k + 1)
            else:
                v = 0
                for r in recs:
                    if r.need_inc:
                        v += r.inc
                    r.val = v
        block = es.enter_context(nc.Block())
        for eng, attr in _ENG_ATTR.items():
            recs = self.per_eng[eng]
            if not recs:
                continue

            def body(proxy, recs=recs, eng=eng):
                pending = {q: [] for q in _QCFG}
                cur_tag = [None, None]

                def set_tag(t):
                    if t == cur_tag[0]:
                        return
                    if cur_tag[0] is not None:
                        nc.leave_named_scope(cur_tag[0], cur_tag[1], False)
                    if t is not None:
                        sid, _ = nc.enter_named_scope(t, False)
                        cur_tag[0], cur_tag[1] = t, sid
                    else:
                        cur_tag[0], cur_tag[1] = None, None

                def emit_echo_n(pq, n):
                    if n <= 0:
                        return
                    for _ in range(n):
                        r = pq.pop(0)
                        k = _QCFG[r.stream][0]
                        proxy.wait_ge(raw[r.stream][r.sidx % k], r.val)
                    proxy.sem_inc(ords[r.stream], n)

                for r in recs:
                    set_tag(r.tag)
                    if r.waits:
                        for q, pq in pending.items():
                            emit_echo_n(pq, len(pq))
                    for p in r.waits:
                        if p.stream in _QCFG:
                            proxy.wait_ge(ords[p.stream], p.sidx + 1)
                        else:
                            proxy.wait_ge(csem[p.stream], p.val)
                    if r.stream in _QCFG:
                        pq = pending[r.stream]
                        k = _QCFG[r.stream][0]
                        inst = r.fn(proxy)
                        inst.then_inc(raw[r.stream][r.sidx % k], r.inc)
                        pq.append(r)
                        if len(pq) > _ECHO_LAG:
                            emit_echo_n(pq, _ECHO_BATCH)
                    else:
                        inst = r.fn(proxy)
                        if r.need_inc:
                            inst.then_inc(csem[r.stream], r.inc)
                for q, pq in pending.items():
                    emit_echo_n(pq, len(pq))
                set_tag(None)
            getattr(block, attr)(body)


class P:

    def __init__(self, nc):
        self.nc = nc
        self.ctx = Ctx(nc)
        self.es = ExitStack()
        self.stacks = [self.es]
        self.n_sb = 0

    def scope(self):
        import contextlib

        @contextlib.contextmanager
        def _s():
            st = ExitStack()
            self.stacks.append(st)
            try:
                with st:
                    yield st
            finally:
                self.stacks.pop()
        return _s()

    def sb(self, name, shape, dt=F32, stack=None):
        h = (stack or self.stacks[-1]).enter_context(self.nc.sbuf_tensor(name, list(shape), dt))
        return T(h, name)

    def ps(self, name, shape, dt=F32):
        h = self.stacks[-1].enter_context(self.nc.psum_tensor(name, list(shape), dt))
        return T(h, name)

    def dram(self, name, shape, dt=F32, kind="Internal"):
        ap = self.nc.dram_tensor(name, list(shape), dt, kind=kind).ap()
        return DT(ap, name)

    _ldq = 0

    def rop(self, eng, method, *args, outs=(), reads=(), dma=False, cc=False, **kw):
        def fn(e, m=method, a=args, k=dict(kw)):
            return getattr(e, m)(*a, **k)
        return self.ctx.op(eng, fn, reads=list(reads), writes=list(outs), dma=dma, cc=cc)

    def load(self, dst_ap, src_ap, dstb, srcb=None, eng=None):
        if eng is None:
            self._ldq += 1
            eng = "sp" if self._ldq % 2 else "act"
        rd = [srcb] if srcb is not None else []
        self.ctx.op(eng, lambda e: e.dma_start(out=dst_ap, in_=src_ap),
                    reads=rd, writes=[dstb], dma=True)

    def store(self, dst_ap, src_ap, srcb, dstb, eng="pool"):
        self.ctx.op(eng, lambda e: e.dma_start(out=dst_ap, in_=src_ap),
                    reads=[srcb], writes=[dstb], dma=True)

    def mm(self, out_ap, lhs_ap, rhs_ap, outb, rbufs, start, stop):
        self.ctx.op("pe", lambda e: e.matmul(out_ap, lhs_ap, rhs_ap, start=start, stop=stop),
                    reads=list(rbufs), writes=[outb])

    def act(self, out_ap, in_ap, func, outb, rbufs, bias=None, scale=None, accum=None, accumb=None):
        kw = {}
        if bias is not None:
            kw["bias"] = bias
        if scale is not None:
            kw["scale"] = scale
        if accum is not None:
            kw["accum_out"] = accum
        wr = [outb] + ([accumb] if accumb is not None else [])
        self.ctx.op("act", lambda e: e.activation(out=out_ap, in_=in_ap, func=func, **kw),
                    reads=list(rbufs), writes=wr)

    def dve(self, fn, outb, rbufs):
        self.ctx.op("dve", fn, reads=list(rbufs), writes=[outb])

    def pool(self, fn, outb, rbufs=()):
        self.ctx.op("pool", fn, reads=list(rbufs), writes=[outb])


def _mk_consts(p):
    c = {}
    c["ones_col_bf"] = p.sb("c_ones_col_bf", [128, 1], BF16)
    c["ones_row_f"] = p.sb("c_ones_row_f", [1, 128], F32)
    c["ones64"] = p.sb("c_ones64", [128, 64], F32)
    c["zero32"] = p.sb("c_zero32", [128, 32], F32)
    c["ones_row_bf"] = p.sb("c_ones_row_bf", [1, 128], BF16)
    c["eps6"] = p.sb("c_eps6", [128, 1])
    c["eps5"] = p.sb("c_eps5", [128, 1])
    c["negC"] = p.sb("c_negC", [128, 1])
    c["c75"] = p.sb("c_c75", [128, 1])
    c["cm4"] = p.sb("c_cm4", [128, 1])
    c["cm10"] = p.sb("c_cm10", [128, 1])
    vals = {"zero32": 0.0, "eps6": 1e-6, "eps5": 1e-5, "negC": -12.0,
            "c75": 7.5, "cm4": -4.0, "cm10": -10.0}
    for k in c:
        val = vals.get(k, 1.0)
        t = c[k]
        p.pool(lambda e, t=t, val=val: e.memset(t.h[:], val), t.b)
    return c


def rmsnorm_fm(p, c, xt, w_tb, wcol, out_put, banks, t_w, sq_bufs, tmp_row):
    stat, bcast = banks
    for kc in range(16):
        tsrc, ap = xt[kc]
        sq = sq_bufs[kc % 2]
        p.act(sq.h[:, :t_w], ap, AF.Square, sq.b, [tsrc.b])
        p.mm(stat.h[0:1, :t_w], c["ones_col_bf"].h[:], sq.h[:, :t_w], stat.b, [sq.b, c["ones_col_bf"].b],
             start=(kc == 0), stop=(kc == 15))
    p.act(tmp_row.h[0:1, :t_w], stat.h[0:1, :t_w], AF.Sqrt, tmp_row.b, [stat.b, c["eps6"].b],
          scale=1.0 / 2048.0, bias=c["eps6"].h[0:1, :])
    p.dve(lambda e: e.reciprocal(tmp_row.h[0:1, :t_w], tmp_row.h[0:1, :t_w]), tmp_row.b, [tmp_row.b])
    p.mm(bcast.h[:, :t_w], c["ones_row_f"].h[:], tmp_row.h[0:1, :t_w], bcast.b,
         [tmp_row.b, c["ones_row_f"].b], start=True, stop=True)
    for kc in range(16):
        tsrc, ap = xt[kc]
        out_put(kc, ap, tsrc, bcast)


def emit_l0_attn(p, c, io, debug=False):
    nc, ctx = p.nc, p.ctx
    b0, b1, b2, b3, b4, b5, b6, b7 = (p.ps(f"A_pb{i}", [128, 512]) for i in range(8))
    pbk = [b0, b1, b2, b3, b4, b5, b6, b7]

    kd = "ExternalOutput" if debug else "Internal"
    aD = p.dram("A_aD", [D, TCORE], F32, kd)
    igD = p.dram("A_igD", [D, TCORE], F32, kd)
    vD = p.dram("A_vD", [D, TCORE], F32, kd)
    cdD = p.dram("A_cdD", [D, TCORE], F32, kd)
    inD = p.dram("A_inD", [D, TCORE], F32, kd)
    recD = p.dram("A_recD", [D, TCORE], BF16, kd)
    xaD = p.dram("A_xaD", [D, TCORE], F32, kd)
    flocD = p.dram("A_flocD", [1, 2048])
    hD = p.dram("A_hD", [D, TCORE], BF16, kd) if debug else None
    wbfD = p.dram("A_wbfD", [D, D], BF16, kd) if debug else None
    flocAG = p.dram("A_flocAG", [2, 2048])

    wguB = p.dram("A_wguB", [D, 16384], BF16)
    wdnB = p.dram("A_wdnB", [8192, D], BF16)
    with ExitStack() as ph:
        wtmp = [p.sb(f"A0_wt{k}", [128, 2048], F32, ph) for k in range(4)]
        wbtmp = [p.sb(f"A0_wb{k}", [128, 2048], BF16, ph) for k in range(4)]
        conv_bf(p, io["wgu"], wguB, D, 16384, wtmp, wbtmp)
        conv_bf(p, io["wdn"], wdnB, 8192, D, wtmp, wbtmp)
        ctx.phase()

    prew = p.sb("A_prew", [128, 16])
    dcb = p.sb("A_dcb", [128, 16])
    ffnw = p.sb("A_ffnw", [128, 16])
    wsel = p.sb("A_wsel", [128, 1])
    for t, key in [(prew, "prew"), (dcb, "dcb"), (ffnw, "ffnw"), (wsel, "wsel")]:
        p.load(t.h[:], io[key].ap[:], t.b, io[key].b)

    with ExitStack() as ph:
        h_sb = [p.sb(f"A_h{k}", [128, TCORE], BF16, ph) for k in range(16)]
        xin = [p.sb(f"A_x{k}", [128, 512], F32, ph) for k in range(16)]
        sqb = [p.sb(f"A_sq{k}", [128, 512], BF16, ph) for k in range(2)]
        rowt = p.sb("A_row", [1, 512], F32, ph)
        for tt in range(4):
            cs = slice(tt * 512, (tt + 1) * 512)
            for kc in range(16):
                p.load(xin[kc].h[:], io["x"].ap[kc * 128:(kc + 1) * 128, cs], xin[kc].b, io["x"].b)

            def put(kc, ap, tsrc, bcast, tt=tt, cs=cs):
                p.dve(lambda e, kc=kc: e.scalar_tensor_tensor(
                    h_sb[kc].h[:, cs], ap, prew.h[:, kc:kc + 1], bcast.h[:, :512],
                    ALU.mult, ALU.mult), h_sb[kc].b, [tsrc.b, prew.b, bcast.b])
            rmsnorm_fm(p, c, [(xin[k], xin[k].h[:]) for k in range(16)], prew, 0,
                       put, (b2, b3), 512, sqb, rowt)
        ctx.phase()
        if debug:
            for kc in range(16):
                p.store(hD.ap[kc * 128:(kc + 1) * 128, :], h_sb[kc].h[:], h_sb[kc].b, hD.b)

        wfp = [p.sb(f"A_wfp{k}", [128, 2048], F32, ph) for k in range(2)]
        wbf = [p.sb(f"A_wbf{k}", [128, 2048], BF16, ph) for k in range(16)]
        ev = [p.sb(f"A_ev{k}", [128, 512], F32, ph) for k in range(2)]
        for wkey, dst, func, bias in [("wa", aD, AF.Sigmoid, dcb), ("wi", igD, AF.Sigmoid, None),
                                      ("wv", vD, None, None)]:
            for kc in range(16):
                f = wfp[kc % 2]
                p.load(f.h[:], io[wkey].ap[kc * 128:(kc + 1) * 128, :], f.b, io[wkey].b)
                p.rop("dve", "tensor_copy", wbf[kc].h[:], f.h[:], outs=[wbf[kc].b], reads=[f.b])
            if debug and wkey == "wa":
                for kc in range(16):
                    p.store(wbfD.ap[kc * 128:(kc + 1) * 128, :], wbf[kc].h[:], wbf[kc].b, wbfD.b)
            for tt in range(4):
                cs = slice(tt * 512, (tt + 1) * 512)
                for oc in range(16):
                    bank = (b0, b1)[oc % 2]
                    for kc in range(16):
                        p.mm(bank.h[:], wbf[kc].h[:, oc * 128:(oc + 1) * 128], h_sb[kc].h[:, cs],
                             bank.b, [wbf[kc].b, h_sb[kc].b], start=(kc == 0), stop=(kc == 15))
                    o = ev[oc % 2]
                    if func is not None:
                        p.act(o.h[:], bank.h[:], func, o.b, [bank.b] + ([bias.b] if bias else []),
                              bias=(bias.h[:, oc:oc + 1] if bias else None))
                    else:
                        p.ctx.op("act", lambda e, o=o, bank=bank: e.copy(out=o.h[:], in_=bank.h[:]),
                                 reads=[bank.b], writes=[o.b])
                    p.store(dst.ap[oc * 128:(oc + 1) * 128, cs], o.h[:], o.b, dst.b)
        ctx.phase()

    ph2 = ExitStack()
    cdl = [p.sb(f"A_cdl{dc}", [128, 32], F32, ph2) for dc in range(16)]
    il = [p.sb(f"A_il{dc}", [128, 32], F32, ph2) for dc in range(16)]
    S0 = [p.sb(f"A_S0{dc}", [128, 32], F32, ph2) for dc in range(16)]
    Pp = [p.sb(f"A_P{dc}", [128, 32], F32, ph2) for dc in range(16)]
    floc = p.sb("A_floc", [128, 16], F32, ph2)
    sinit = p.sb("A_sinit", [128, 16], F32, ph2)
    with ExitStack() as ph:
        at = p.sb("A_at", [128, 2048], F32, ph)
        igt = p.sb("A_igt", [128, 2048], F32, ph)
        vt = p.sb("A_vt", [128, 2048], F32, ph)
        w1 = p.sb("A_w1", [128, 2048], F32, ph)
        w2 = p.sb("A_w2", [128, 2048], F32, ph)
        w3 = p.sb("A_w3", [128, 2048], F32, ph)
        for dc in range(16):
            rs = slice(dc * 128, (dc + 1) * 128)
            p.load(at.h[:], aD.ap[rs, :], at.b, aD.b)
            p.load(igt.h[:], igD.ap[rs, :], igt.b, igD.b)
            p.load(vt.h[:], vD.ap[rs, :], vt.b, vD.b)
            p.act(w1.h[:], at.h[:], AF.Square, w1.b, [at.b])
            p.dve(lambda e: e.tensor_scalar(w1.h[:], w1.h[:], -1.0, 1.0, ALU.mult, ALU.add), w1.b, [w1.b])
            p.dve(lambda e: e.tensor_scalar_max(w1.h[:], w1.h[:], 1e-8), w1.b, [w1.b])
            p.act(w1.h[:], w1.h[:], AF.Sqrt, w1.b, [w1.b])
            p.dve(lambda e: e.tensor_mul(w2.h[:], igt.h[:], vt.h[:]), w2.b, [igt.b, vt.b])
            p.dve(lambda e: e.tensor_mul(w2.h[:], w2.h[:], w1.h[:]), w2.b, [w2.b, w1.b])
            p.dve(lambda e: e.tensor_scalar_max(w1.h[:], at.h[:], 1e-10), w1.b, [at.b])
            p.act(w1.h[:], w1.h[:], AF.Ln, w1.b, [w1.b])
            for cc_ in range(32):
                s = slice(cc_ * 64, (cc_ + 1) * 64)
                p.dve(lambda e, s=s: e.tensor_tensor_scan(w3.h[:, s], c["ones64"].h[:], w1.h[:, s],
                                                          0.0, ALU.mult, ALU.add), w3.b,
                      [w1.b, c["ones64"].b])
            p.act(w3.h[:], w3.h[:], AF.Exp, w3.b, [w3.b])
            p.store(cdD.ap[rs, :], w3.h[:], w3.b, cdD.b)
            p.dve(lambda e: e.tensor_scalar_max(w1.h[:], w3.h[:], 1e-10), w1.b, [w3.b])
            p.dve(lambda e: e.reciprocal(w1.h[:], w1.h[:]), w1.b, [w1.b])
            p.dve(lambda e: e.tensor_mul(w2.h[:], w2.h[:], w1.h[:]), w2.b, [w2.b, w1.b])
            for cc_ in range(32):
                s = slice(cc_ * 64, (cc_ + 1) * 64)
                p.dve(lambda e, s=s: e.tensor_tensor_scan(w1.h[:, s], c["ones64"].h[:], w2.h[:, s],
                                                          0.0, ALU.mult, ALU.add), w1.b,
                      [w2.b, c["ones64"].b])
            p.dve(lambda e: e.tensor_mul(w1.h[:], w1.h[:], w3.h[:]), w1.b, [w1.b, w3.b])
            p.store(inD.ap[rs, :], w1.h[:], w1.b, inD.b)
            p.dve(lambda e, dc=dc: e.tensor_copy(cdl[dc].h[:], w3.h[:, 63:2048:64]), cdl[dc].b, [w3.b])
            p.dve(lambda e, dc=dc: e.tensor_copy(il[dc].h[:], w1.h[:, 63:2048:64]), il[dc].b, [w1.b])
            p.dve(lambda e, dc=dc: e.tensor_tensor_scan(S0[dc].h[:], cdl[dc].h[:], il[dc].h[:],
                                                        0.0, ALU.mult, ALU.add), S0[dc].b,
                  [cdl[dc].b, il[dc].b])
            p.dve(lambda e, dc=dc: e.tensor_tensor_scan(Pp[dc].h[:], cdl[dc].h[:], c["zero32"].h[:],
                                                        1.0, ALU.mult, ALU.add), Pp[dc].b,
                  [cdl[dc].b, c["zero32"].b])
            p.dve(lambda e, dc=dc: e.tensor_copy(floc.h[:, dc:dc + 1], S0[dc].h[:, 31:32]),
                  floc.b, [S0[dc].b])
        ctx.phase()
    p.store(flocD.ap[:], floc.h[:], floc.b, flocD.b)
    p.ctx.op("pool", lambda e: e.collective_compute(
        "AllGather", ALU.bypass, replica_groups=[[0, 1], [2, 3], [4, 5], [6, 7]],
        ins=[flocD.ap.opt()], outs=[flocAG.ap.opt()]),
        reads=[flocD.b], writes=[flocAG.b], cc=True)
    p.load(sinit.h[:], flocAG.ap[0:1, :].rearrange("o (p c) -> (o p) c", p=128), sinit.b, flocAG.b)
    p.dve(lambda e: e.tensor_scalar(sinit.h[:], sinit.h[:], wsel.h[:], None, ALU.mult),
          sinit.b, [sinit.b, wsel.b])

    with ExitStack() as ph:
        cdt = p.sb("A_cdt", [128, 2048], F32, ph)
        int_ = p.sb("A_int", [128, 2048], F32, ph)
        Sin = p.sb("A_Sin", [128, 32], F32, ph)
        Pin = p.sb("A_Pin", [128, 32], F32, ph)
        tmp = p.sb("A_tmp2", [128, 32], F32, ph)
        recb = p.sb("A_recb", [128, 2048], BF16, ph)
        ctmp = p.sb("A_ctmp", [128, 2048], F32, ph)
        for dc in range(16):
            rs = slice(dc * 128, (dc + 1) * 128)
            p.load(cdt.h[:], cdD.ap[rs, :], cdt.b, cdD.b)
            p.load(int_.h[:], inD.ap[rs, :], int_.b, inD.b)
            p.pool(lambda e: e.memset(Pin.h[:, 0:1], 1.0), Pin.b)
            p.dve(lambda e, dc=dc: e.tensor_copy(Pin.h[:, 1:32], Pp[dc].h[:, 0:31]), Pin.b,
                  [Pp[dc].b, Pin.b])
            p.pool(lambda e: e.memset(Sin.h[:, 0:1], 0.0), Sin.b)
            p.dve(lambda e, dc=dc: e.tensor_copy(Sin.h[:, 1:32], S0[dc].h[:, 0:31]), Sin.b,
                  [S0[dc].b, Sin.b])
            p.dve(lambda e, dc=dc: e.tensor_scalar(tmp.h[:], Pin.h[:], sinit.h[:, dc:dc + 1],
                                                   None, ALU.mult), tmp.b, [Pin.b, sinit.b])
            p.dve(lambda e: e.tensor_add(tmp.h[:], tmp.h[:], Sin.h[:]), tmp.b, [tmp.b, Sin.b])
            bc = bass.AP(tmp.h, 0, [[32, 128], [1, 32], [0, 64]])
            p.dve(lambda e, bc=bc: e.tensor_tensor(
                ctmp.h.ap().rearrange("p (c j) -> p c j", c=32), cdt.h.ap().rearrange(
                    "p (c j) -> p c j", c=32), bc, ALU.mult), ctmp.b, [cdt.b, tmp.b])
            p.dve(lambda e: e.tensor_add(recb.h[:], int_.h[:], ctmp.h[:]), recb.b, [int_.b, ctmp.b])
            p.store(recD.ap[rs, :], recb.h[:], recb.b, recD.b)
        ctx.phase()
    ph2.close()

    with ExitStack() as ph:
        wfp = [p.sb(f"A3_wfp{k}", [128, 2048], F32, ph) for k in range(2)]
        wbf = [p.sb(f"A3_wbf{k}", [128, 2048], BF16, ph) for k in range(16)]
        rsl = [p.sb(f"A3_r{k}", [128, 512], BF16, ph) for k in range(16)]
        xsl = [p.sb(f"A3_x{k}", [128, 512], F32, ph) for k in range(2)]
        ev = [p.sb(f"A3_ev{k}", [128, 512], F32, ph) for k in range(2)]
        for kc in range(16):
            f = wfp[kc % 2]
            p.load(f.h[:], io["wo"].ap[kc * 128:(kc + 1) * 128, :], f.b, io["wo"].b)
            p.rop("dve", "tensor_copy", wbf[kc].h[:], f.h[:], outs=[wbf[kc].b], reads=[f.b])
        for tt in range(4):
            cs = slice(tt * 512, (tt + 1) * 512)
            for kc in range(16):
                p.load(rsl[kc].h[:], recD.ap[kc * 128:(kc + 1) * 128, cs], rsl[kc].b, recD.b)
            for oc in range(16):
                bank = (b0, b1)[oc % 2]
                for kc in range(16):
                    p.mm(bank.h[:], wbf[kc].h[:, oc * 128:(oc + 1) * 128], rsl[kc].h[:],
                         bank.b, [wbf[kc].b, rsl[kc].b], start=(kc == 0), stop=(kc == 15))
                x0 = xsl[oc % 2]
                p.load(x0.h[:], io["x"].ap[oc * 128:(oc + 1) * 128, cs], x0.b, io["x"].b)
                o = ev[oc % 2]
                p.dve(lambda e, o=o, x0=x0, bank=bank: e.tensor_add(o.h[:], x0.h[:], bank.h[:]),
                      o.b, [x0.b, bank.b])
                p.store(xaD.ap[oc * 128:(oc + 1) * 128, cs], o.h[:], o.b, xaD.b)
        ctx.phase()

    with ExitStack() as ph:
        xa = [p.sb(f"A4_xa{k}", [128, 512], F32, ph) for k in range(16)]
        h2 = [p.sb(f"A4_h2{k}", [128, 512], BF16, ph) for k in range(16)]
        sqb = [p.sb(f"A4_sq{k}", [128, 512], BF16, ph) for k in range(2)]
        rowt = p.sb("A4_row", [1, 512], F32, ph)
        rowt2 = p.sb("A4_row2", [1, 512], F32, ph)
        wpool = [p.sb(f"A4_wp{k}", [128, 1024], BF16, ph) for k in range(16)]
        gsb = [p.sb(f"A4_gs{k}", [128, 512], F32, ph) for k in range(8)]
        inter = [p.sb(f"A4_in{k}", [128, 512], BF16, ph) for k in range(8)]
        ybf = [p.sb(f"A4_ybf{k}", [128, 512], BF16, ph) for k in range(2)]
        ev = [p.sb(f"A4_ev{k}", [128, 512], F32, ph) for k in range(2)]
        wi_ = [0]

        def wnext():
            wi_[0] += 1
            return wpool[wi_[0] % 16]
        for tt in range(4):
            cs = slice(tt * 512, (tt + 1) * 512)
            for kc in range(16):
                p.load(xa[kc].h[:], xaD.ap[kc * 128:(kc + 1) * 128, cs], xa[kc].b, xaD.b)

            def put(kc, ap, tsrc, bcast):
                p.dve(lambda e, kc=kc: e.scalar_tensor_tensor(
                    h2[kc].h[:], ap, ffnw.h[:, kc:kc + 1], bcast.h[:, :512],
                    ALU.mult, ALU.mult), h2[kc].b, [tsrc.b, ffnw.b, bcast.b])
            rmsnorm_fm(p, c, [(xa[k], xa[k].h[:]) for k in range(16)], ffnw, 0,
                       put, (b2, b3), 512, sqb, rowt)
            for g in range(8):
                for kc in range(16):
                    f = wnext()
                    p.load(f.h[:], wguB.ap[kc * 128:(kc + 1) * 128,
                                           g * 1024:(g + 1) * 1024], f.b, wguB.b)
                    for j in range(8):
                        p.mm(pbk[j].h[:], f.h[:, j * 128:(j + 1) * 128], h2[kc].h[:],
                             pbk[j].b, [f.b, h2[kc].b], start=(kc == 0), stop=(kc == 15))
                for j in range(8):
                    p.act(gsb[j].h[:], pbk[j].h[:], AF.Silu, gsb[j].b, [pbk[j].b])
                for kc in range(16):
                    f = wnext()
                    p.load(f.h[:], wguB.ap[kc * 128:(kc + 1) * 128,
                                           8192 + g * 1024:8192 + (g + 1) * 1024], f.b, wguB.b)
                    for j in range(8):
                        p.mm(pbk[j].h[:], f.h[:, j * 128:(j + 1) * 128], h2[kc].h[:],
                             pbk[j].b, [f.b, h2[kc].b], start=(kc == 0), stop=(kc == 15))
                for j in range(8):
                    p.rop("dve", "tensor_mul", inter[j].h[:], gsb[j].h[:], pbk[j].h[:],
                          outs=[inter[j].b], reads=[gsb[j].b, pbk[j].b])
                for half in range(2):
                    for j in range(8):
                        f = wnext()
                        p.load(f.h[:], wdnB.ap[g * 1024 + j * 128:g * 1024 + (j + 1) * 128,
                                               half * 1024:(half + 1) * 1024], f.b, wdnB.b)
                        for oc in range(8):
                            p.mm(pbk[oc].h[:], f.h[:, oc * 128:(oc + 1) * 128], inter[j].h[:],
                                 pbk[oc].b, [f.b, inter[j].b], start=(j == 0), stop=(j == 7))
                    for oc in range(8):
                        o = half * 8 + oc
                        p.rop("dve", "tensor_add", xa[o].h[:], xa[o].h[:], pbk[oc].h[:],
                              outs=[xa[o].b], reads=[xa[o].b, pbk[oc].b])
            for oc in range(16):
                xv = ev[oc % 2]
                p.load(xv.h[:], io["x"].ap[oc * 128:(oc + 1) * 128, cs], xv.b, io["x"].b)
                p.dve(lambda e, oc=oc, xv=xv: e.tensor_add(xa[oc].h[:], xa[oc].h[:], xv.h[:]),
                      xa[oc].b, [xa[oc].b, xv.b])

            def putA(oc, bm, br, cs=cs):
                o = ev[oc % 2]
                p.rop("dve", "tensor_sub", o.h[:], xa[oc].h[:], bm.h[:], outs=[o.b],
                      reads=[xa[oc].b, bm.b])
                p.rop("dve", "tensor_mul", o.h[:], o.h[:], br.h[:], outs=[o.b], reads=[o.b, br.b])
                p.store(io["x1"].ap[oc * 128:(oc + 1) * 128, cs], o.h[:], o.b, io["x1"].b)
            ln_fm(p, c, xa, (b2, b3, b4, b5), [sqb[0], sqb[1], ybf[0], ybf[1]],
                  rowt, rowt2, putA)
        ctx.phase()


def build_launch_A(debug=False):
    nc = bass.Bass("TRN2", target_bir_lowering=False, debug=False)
    p = P(nc)
    io = {}
    io["x"] = DT(nc.dram_tensor("x", [D, TCORE], F32, kind="ExternalInput").ap(), "x")
    for nm, shp in [("wa", [D, D]), ("wi", [D, D]), ("wv", [D, D]), ("wo", [D, D]),
                    ("wgu", [D, 16384]), ("wdn", [8192, D])]:
        io[nm] = DT(nc.dram_tensor(nm, shp, F32, kind="ExternalInput").ap(), nm)
    for nm in ["prew", "dcb", "ffnw"]:
        io[nm] = DT(nc.dram_tensor(nm, [128, 16], F32, kind="ExternalInput").ap(), nm)
    io["wsel"] = DT(nc.dram_tensor("wsel", [128, 1], F32, kind="ExternalInput").ap(), "wsel")
    io["x1"] = DT(nc.dram_tensor("x1", [D, TCORE], F32, kind="ExternalOutput").ap(), "x1")
    with p.es:
        c = _mk_consts(p)
        emit_l0_attn(p, c, io, debug=debug)
        p.ctx.finish()
        p.ctx.build(p.es)
    return nc




def _slices_fm(x):
    if isinstance(x, list):
        return x
    out = []
    for cid in range(NCORE):
        b, h = cid // 2, cid % 2
        out.append(np.ascontiguousarray(np.asarray(x)[b, h * 2048:(h + 1) * 2048, :].T))
    return out


def _fm(w):
    return np.ascontiguousarray(w.reshape(-1, 128).T.astype(np.float32))


def prep_inputs_A(ins):
    x = np.asarray(ins["L0_attn_x"], np.float32)
    maps = []
    for cid in range(NCORE):
        b, h = cid // 2, cid % 2
        xs = np.ascontiguousarray(x[b, h * 2048:(h + 1) * 2048, :].T)
        maps.append({
            "x": xs,
            "wa": np.asarray(ins["L0_attn_w_a"], np.float32),
            "wi": np.asarray(ins["L0_attn_w_i"], np.float32),
            "wv": np.asarray(ins["L0_attn_w_v"], np.float32),
            "wo": np.asarray(ins["L0_attn_out_proj_w"], np.float32),
            "wgu": np.asarray(ins["L0_attn_w_gate_up"], np.float32),
            "wdn": np.asarray(ins["L0_attn_w_down"], np.float32),
            "prew": _fm(np.asarray(ins["L0_attn_pre_norm_w"], np.float32)),
            "dcb": _fm(np.asarray(ins["L0_attn_decay_bias"], np.float32)),
            "ffnw": _fm(np.asarray(ins["L0_attn_ffn_norm_w"], np.float32)),
            "wsel": np.full((128, 1), float(h), np.float32),
        })
    return maps


_NC_CACHE = {}


def _nc_for(name, builder):
    if name not in _NC_CACHE:
        _NC_CACHE[name] = builder()
    return _NC_CACHE[name]


def kernel_multi(**inputs):
    cores = list(range(NCORE))
    resA = run_bass_kernel_spmd(_nc_for("A", build_launch_A), prep_inputs_A(inputs),
                                core_ids=cores)
    x1c = [np.asarray(resA.results[c]["x1"]) for c in cores]
    resB = run_bass_kernel_spmd(_nc_for("B", build_launch_B), prep_inputs_B(inputs, x1c),
                                core_ids=cores)
    x2c = [np.asarray(resB.results[c]["x2"]) for c in cores]
    resC = run_bass_kernel_spmd(_nc_for("C", build_launch_C), prep_inputs_C(inputs, x2c),
                                core_ids=cores)
    x3c = [np.asarray(resC.results[c]["x3"]) for c in cores]
    resD = run_bass_kernel_spmd(_nc_for("D", build_launch_D), prep_inputs_D(inputs, x3c),
                                core_ids=cores)
    out = np.empty((4, 4096, 2048), np.float32)
    for cid in cores:
        b, h = cid // 2, cid % 2
        out[b, h * 2048:(h + 1) * 2048, :] = resD.results[cid]["out"].T
    return out


HID = [1024, 2048, 3072, 4096, 1024, 2048, 3072, 4096]


def ln_fm(p, c, y, banks, sqb, rowt, rowt2, put, t_w=512):
    s1, s2, bm, br = banks
    for kc in range(16):
        yb = sqb[0] if kc % 2 == 0 else sqb[1]
        p.rop("dve", "tensor_copy", yb.h[:, :t_w], y[kc].h[:, :t_w], outs=[yb.b], reads=[y[kc].b])
        p.mm(s1.h[0:1, :t_w], c["ones_col_bf"].h[:], yb.h[:, :t_w], s1.b,
             [yb.b, c["ones_col_bf"].b], start=(kc == 0), stop=(kc == 15))
        sq = sqb[2] if kc % 2 == 0 else sqb[3]
        p.act(sq.h[:, :t_w], y[kc].h[:, :t_w], AF.Square, sq.b, [y[kc].b])
        p.mm(s2.h[0:1, :t_w], c["ones_col_bf"].h[:], sq.h[:, :t_w], s2.b,
             [sq.b, c["ones_col_bf"].b], start=(kc == 0), stop=(kc == 15))
    p.rop("dve", "tensor_scalar_mul", rowt.h[0:1, :t_w], s1.h[0:1, :t_w], 1.0 / 2048.0,
          outs=[rowt.b], reads=[s1.b])
    p.rop("dve", "scalar_tensor_tensor", rowt2.h[0:1, :t_w], rowt.h[0:1, :t_w], 1.0,
          rowt.h[0:1, :t_w], ALU.mult, ALU.mult, outs=[rowt2.b], reads=[rowt.b])
    p.rop("dve", "scalar_tensor_tensor", rowt2.h[0:1, :t_w], s2.h[0:1, :t_w], 1.0 / 2048.0,
          rowt2.h[0:1, :t_w], ALU.mult, ALU.subtract, outs=[rowt2.b], reads=[s2.b, rowt2.b])
    p.act(rowt2.h[0:1, :t_w], rowt2.h[0:1, :t_w], AF.Sqrt, rowt2.b, [rowt2.b, c["eps5"].b],
          bias=c["eps5"].h[0:1, :])
    p.rop("dve", "reciprocal", rowt2.h[0:1, :t_w], rowt2.h[0:1, :t_w],
          outs=[rowt2.b], reads=[rowt2.b])
    p.mm(bm.h[:, :t_w], c["ones_row_f"].h[:], rowt.h[0:1, :t_w], bm.b,
         [rowt.b, c["ones_row_f"].b], start=True, stop=True)
    p.mm(br.h[:, :t_w], c["ones_row_f"].h[:], rowt2.h[0:1, :t_w], br.b,
         [rowt2.b, c["ones_row_f"].b], start=True, stop=True)
    for oc in range(16):
        put(oc, bm, br)


def conv_bf(p, src, dst, R, C, wtmp, wbtmp):
    i = 0
    for r0 in range(0, R, 128):
        for c0 in range(0, C, 2048):
            cw_ = min(2048, C - c0)
            f = wtmp[i % len(wtmp)]
            g = wbtmp[i % len(wbtmp)]
            i += 1
            p.load(f.h[:, :cw_], src.ap[r0:r0 + 128, c0:c0 + cw_], f.b, src.b)
            p.rop("dve", "tensor_copy", g.h[:, :cw_], f.h[:, :cw_], outs=[g.b], reads=[f.b])
            p.store(dst.ap[r0:r0 + 128, c0:c0 + cw_], g.h[:, :cw_], g.b, dst.b)


def emit_l0_ffn(p, c, io, debug=False):
    ctx = p.ctx
    pb = [p.ps(f"B_pb{i}", [128, 512]) for i in range(8)]

    wB = {}
    shapes = {"dw": (2048, 1024), "uw": (1024, 2048), "s1": (1024, 1024),
              "s2": (1024, 1024), "cw": (2048, 2048)}
    for i, hh in enumerate(HID):
        shapes[f"e{i}w1"] = (1024, hh)
        shapes[f"e{i}w2"] = (hh, 1024)
    for k, (R, C) in shapes.items():
        wB[k] = p.dram(f"B_{k}B", [R, C], BF16)

    rwsb = [p.sb(f"B_rw{kc}", [128, 8], BF16) for kc in range(16)]
    dbf = p.sb("B_dbf", [128, 8])
    sb1f = p.sb("B_sb1f", [128, 8])
    b1f = [p.sb(f"B_b1f{i}", [128, HID[i] // 128]) for i in range(8)]
    sb2s = p.sb("B_sb2s", [1, 1024])
    ubr = p.sb("B_ubr", [1, 2048])
    cbr = p.sb("B_cbr", [1, 2048])
    rbr = p.sb("B_rbr", [1, 8])
    negc = p.sb("B_negc", [1, 8])
    tbeps = p.sb("B_tbeps", [128, 8])
    idf = p.sb("B_idf", [128, 128])
    p.load(dbf.h[:], io["dbf"].ap[:], dbf.b, io["dbf"].b)
    p.load(sb1f.h[:], io["sb1f"].ap[:], sb1f.b, io["sb1f"].b)
    for i in range(8):
        p.load(b1f[i].h[:], io[f"e{i}b1f"].ap[:], b1f[i].b, io[f"e{i}b1f"].b)
    p.load(sb2s.h[:], io["sb2r"].ap[:], sb2s.b, io["sb2r"].b)
    p.rop("dve", "tensor_scalar_mul", sb2s.h[:], sb2s.h[:], 0.1, outs=[sb2s.b], reads=[sb2s.b])
    for t, key in [(ubr, "ubr"), (cbr, "cbr"), (rbr, "rbr"), (negc, "negc"),
                   (tbeps, "tbeps"), (idf, "idf")]:
        p.load(t.h[:], io[key].ap[:], t.b, io[key].b)
    c["ones_row512"] = p.sb("c_ones_row512", [1, 512])
    t_ = c["ones_row512"]
    p.rop("pool", "memset", t_.h[:], 1.0, outs=[t_.b])

    with ExitStack() as ph:
        wtmp = [p.sb(f"B0_wt{k}", [128, 2048], F32, ph) for k in range(2)]
        wbtmp = [p.sb(f"B0_wb{k}", [128, 2048], BF16, ph) for k in range(2)]
        for k, (R, C) in shapes.items():
            conv_bf(p, io[k], wB[k], R, C, wtmp, wbtmp)
        for kc in range(16):
            f = wtmp[kc % 2]
            p.load(f.h[:, :8], io["rw"].ap[kc * 128:(kc + 1) * 128, :], f.b, io["rw"].b)
            p.rop("dve", "tensor_copy", rwsb[kc].h[:], f.h[:, :8], outs=[rwsb[kc].b], reads=[f.b])
        ctx.phase()

    with ExitStack() as ph:
        x = [p.sb(f"B_x{k}", [128, 512], F32, ph) for k in range(16)]
        poolA = [p.sb(f"B_pA{k}", [128, 512], BF16, ph) for k in range(32)]
        poolB = [p.sb(f"B_pB{k}", [128, 512], BF16, ph) for k in range(24)]
        xbf = poolA[0:16]
        g1s = poolA
        G = poolB[0:8]
        ub = poolB[8:16]
        Gx = poolB[0:16]
        ombf = poolB[16:24]
        macc = [p.sb(f"B_macc{k}", [128, 512], F32, ph) for k in range(8)]
        gb = [p.sb(f"B_gb{k}", [128, 512], F32, ph) for k in range(2)]
        mrow = [[p.sb(f"B_mr{kk}_{e}", [1, 512], BF16, ph) for e in range(8)] for kk in range(2)]
        grow = [p.sb(f"B_gr{kk}", [1, 512], F32, ph) for kk in range(2)]
        w1rot = [p.sb(f"B_w1r{k}", [128, 1024], BF16, ph) for k in range(4)]
        w2rot = [p.sb(f"B_w2r{k}", [128, 1024], BF16, ph) for k in range(4)]
        wrot = [p.sb(f"B_wr{k}", [128, 1024], BF16, ph) for k in range(4)]
        b2rot = [p.sb(f"B_b2r{k}_", [1, 1024], F32, ph) for k in range(2)]
        gact = [p.sb(f"B_gact{k}", [128, 512], F32, ph) for k in range(2)]
        sqb = [p.sb(f"B_sq{k}", [128, 512], BF16, ph) for k in range(4)]
        rowt = p.sb("B_row", [1, 512], F32, ph)
        rowt2 = p.sb("B_row2", [1, 512], F32, ph)
        ev = [p.sb(f"B_ev{k}", [128, 512], F32, ph) for k in range(2)]
        r8 = [p.sb(f"B_r8{k}", [128, 8], F32, ph) for k in range(4)]
        r1 = [p.sb(f"B_r1{k}", [128, 1], F32, ph) for k in range(6)]

        for tt in range(4):
            cs = slice(tt * 512, (tt + 1) * 512)
            for kc in range(16):
                p.load(x[kc].h[:], io["x1"].ap[kc * 128:(kc + 1) * 128, cs], x[kc].b, io["x1"].b)
                p.rop("dve", "tensor_copy", xbf[kc].h[:], x[kc].h[:], outs=[xbf[kc].b],
                      reads=[x[kc].b])
            i = 0
            for kc in range(16):
                f = wrot[kc % 4]
                p.load(f.h[:, :1024], wB["dw"].ap[kc * 128:(kc + 1) * 128, :], f.b, wB["dw"].b)
                for oc in range(8):
                    p.mm(pb[oc].h[:], f.h[:, oc * 128:(oc + 1) * 128], xbf[kc].h[:],
                         pb[oc].b, [f.b, xbf[kc].b], start=(kc == 0), stop=(kc == 15))
            for oc in range(8):
                p.act(G[oc].h[:], pb[oc].h[:], AF.Gelu, G[oc].b, [pb[oc].b, dbf.b],
                      bias=dbf.h[:, oc:oc + 1])
            for ts in range(4):
                tsl = slice(ts * 128, (ts + 1) * 128)
                pr = pb[0]
                for kc in range(16):
                    p.mm(pr.h[:, :8], xbf[kc].h[:, tsl], rwsb[kc].h[:], pr.b,
                         [xbf[kc].b, rwsb[kc].b], start=(kc == 0), stop=False)
                p.mm(pr.h[:, :8], c["ones_row_f"].h[:], rbr.h[:], pr.b,
                     [c["ones_row_f"].b, rbr.b], start=False, stop=False)
                p.mm(pr.h[:, :8], c["ones_row_f"].h[:], negc.h[:], pr.b,
                     [c["ones_row_f"].b, negc.b], start=False, stop=True)
                pn, tb, m1c, msk = r8[0], r8[1], r1[0], r8[2]
                p.rop("dve", "reduce_max", m1c.h[:], pr.h[:, :8], AX.X,
                      outs=[m1c.b], reads=[pr.b])
                p.rop("dve", "tensor_scalar_mul", m1c.h[:], m1c.h[:], -1.0,
                      outs=[m1c.b], reads=[m1c.b])
                p.act(pn.h[:], pr.h[:, :8], AF.Exp, pn.b, [pr.b, m1c.b],
                      bias=m1c.h[:], accum=r1[1].h[:], accumb=r1[1].b)
                p.rop("dve", "reciprocal", r1[1].h[:], r1[1].h[:], outs=[r1[1].b], reads=[r1[1].b])
                p.rop("dve", "tensor_scalar", pn.h[:], pn.h[:], r1[1].h[:], None, ALU.mult,
                      outs=[pn.b], reads=[pn.b, r1[1].b])
                p.rop("dve", "tensor_add", tb.h[:], pn.h[:], tbeps.h[:], outs=[tb.b],
                      reads=[pn.b, tbeps.b])
                for kk in range(2):
                    mx, tv = r1[2 + kk], r1[4 + kk]
                    p.rop("dve", "reduce_max", mx.h[:], tb.h[:], AX.X, outs=[mx.b], reads=[tb.b])
                    p.rop("dve", "tensor_scalar", msk.h[:], tb.h[:], mx.h[:], None, ALU.is_ge,
                          outs=[msk.b], reads=[tb.b, mx.b])
                    p.rop("dve", "tensor_mul", r8[3].h[:], msk.h[:], pn.h[:],
                          outs=[r8[3].b], reads=[msk.b, pn.b])
                    p.rop("dve", "reduce_sum", tv.h[:], r8[3].h[:], AX.X, outs=[tv.b],
                          reads=[r8[3].b])
                    if kk == 0:
                        p.rop("dve", "scalar_tensor_tensor", tb.h[:], msk.h[:], -1e9, tb.h[:],
                              ALU.mult, ALU.add, outs=[tb.b], reads=[msk.b, tb.b])
                    for e in range(8):
                        p.ctx.op("pe", (lambda ee, o=pb[6].h[0:1, :128], i_=msk.h[:, e:e + 1],
                                        idf_=idf.h[:]: ee.transpose(o, i_, idf_)),
                                 reads=[msk.b, idf.b], writes=[pb[6].b])
                        p.rop("dve", "tensor_copy", mrow[kk][e].h[0:1, tsl], pb[6].h[0:1, :128],
                              outs=[mrow[kk][e].b], reads=[pb[6].b])
                g1c, g2c = r1[2], r1[3]
                p.act(g1c.h[:], r1[4].h[:], AF.Exp, g1c.b, [r1[4].b])
                p.act(g2c.h[:], r1[5].h[:], AF.Exp, g2c.b, [r1[5].b])
                p.rop("dve", "tensor_add", r1[1].h[:], g1c.h[:], g2c.h[:], outs=[r1[1].b],
                      reads=[g1c.b, g2c.b])
                p.rop("dve", "reciprocal", r1[1].h[:], r1[1].h[:], outs=[r1[1].b], reads=[r1[1].b])
                for kk, gc in enumerate([g1c, g2c]):
                    p.rop("dve", "tensor_mul", gc.h[:], gc.h[:], r1[1].h[:], outs=[gc.b],
                          reads=[gc.b, r1[1].b])
                    p.ctx.op("pe", (lambda ee, o=pb[6].h[0:1, :128], i_=gc.h[:],
                                    idf_=idf.h[:]: ee.transpose(o, i_, idf_)),
                             reads=[gc.b, idf.b], writes=[pb[6].b])
                    p.rop("dve", "tensor_copy", grow[kk].h[0:1, tsl], pb[6].h[0:1, :128],
                          outs=[grow[kk].b], reads=[pb[6].b])
            for kk in range(2):
                p.mm(pb[7].h[:], c["ones_row_f"].h[:], grow[kk].h[:], pb[7].b,
                     [c["ones_row_f"].b, grow[kk].b], start=True, stop=True)
                p.rop("dve", "tensor_copy", gb[kk].h[:], pb[7].h[:], outs=[gb[kk].b],
                      reads=[pb[7].b])
            first_acc = [True]
            for kk in range(2):
                for e in range(8):
                    hh = HID[e]
                    noh = hh // 128
                    p.mm(pb[7].h[:], c["ones_row_bf"].h[:], mrow[kk][e].h[:], pb[7].b,
                         [c["ones_row_bf"].b, mrow[kk][e].b], start=True, stop=True)
                    for kc in range(8):
                        p.rop("dve", "tensor_mul", ub[kc].h[:], G[kc].h[:], pb[7].h[:],
                              outs=[ub[kc].b], reads=[G[kc].b, pb[7].b])
                    for g0 in range(0, noh, 8):
                        gn = min(8, noh - g0)
                        for kc in range(8):
                            f = w1rot[kc % 4]
                            p.load(f.h[:, :gn * 128],
                                   wB[f"e{e}w1"].ap[kc * 128:(kc + 1) * 128,
                                                    g0 * 128:(g0 + gn) * 128],
                                   f.b, wB[f"e{e}w1"].b)
                            for j in range(gn):
                                p.mm(pb[j].h[:], f.h[:, j * 128:(j + 1) * 128], ub[kc].h[:],
                                     pb[j].b, [f.b, ub[kc].b], start=(kc == 0), stop=(kc == 7))
                        for j in range(gn):
                            oh = g0 + j
                            ga = gact[j % 2]
                            p.act(ga.h[:], pb[j].h[:], AF.Gelu, ga.b, [pb[j].b, b1f[e].b],
                                  bias=b1f[e].h[:, oh:oh + 1])
                            p.rop("dve", "tensor_mul", g1s[oh].h[:], ga.h[:], gb[kk].h[:],
                                  outs=[g1s[oh].b], reads=[ga.b, gb[kk].b])
                    for oh in range(noh):
                        f = w2rot[oh % 4]
                        p.load(f.h[:], wB[f"e{e}w2"].ap[oh * 128:(oh + 1) * 128, :], f.b,
                               wB[f"e{e}w2"].b)
                        for oc in range(8):
                            p.mm(pb[oc].h[:], f.h[:, oc * 128:(oc + 1) * 128], g1s[oh].h[:],
                                 pb[oc].b, [f.b, g1s[oh].b], start=(oh == 0), stop=False)
                    b2t = b2rot[(kk * 8 + e) % 2]
                    p.load(b2t.h[:], io[f"e{e}b2r"].ap[:], b2t.b, io[f"e{e}b2r"].b)
                    for oc in range(8):
                        p.mm(pb[oc].h[:], b2t.h[0:1, oc * 128:(oc + 1) * 128], grow[kk].h[:],
                             pb[oc].b, [b2t.b, grow[kk].b], start=False, stop=True)
                    for oc in range(8):
                        if first_acc[0]:
                            p.rop("dve", "tensor_copy", macc[oc].h[:], pb[oc].h[:],
                                  outs=[macc[oc].b], reads=[pb[oc].b])
                        else:
                            p.rop("dve", "tensor_add", macc[oc].h[:], macc[oc].h[:], pb[oc].h[:],
                                  outs=[macc[oc].b], reads=[macc[oc].b, pb[oc].b])
                    first_acc[0] = False
            for kc in range(8):
                f = w1rot[kc % 4]
                p.load(f.h[:, :1024], wB["s1"].ap[kc * 128:(kc + 1) * 128, :],
                       f.b, wB["s1"].b)
                for j in range(8):
                    p.mm(pb[j].h[:], f.h[:, j * 128:(j + 1) * 128], G[kc].h[:],
                         pb[j].b, [f.b, G[kc].b], start=(kc == 0), stop=(kc == 7))
            for j in range(8):
                oh = j
                ga = gact[j % 2]
                p.act(ga.h[:], pb[j].h[:], AF.Gelu, ga.b, [pb[j].b, sb1f.b],
                      bias=sb1f.h[:, oh:oh + 1])
                p.rop("dve", "tensor_scalar_mul", g1s[oh].h[:], ga.h[:], 0.1,
                      outs=[g1s[oh].b], reads=[ga.b])
            for oh in range(8):
                f = w2rot[oh % 4]
                p.load(f.h[:], wB["s2"].ap[oh * 128:(oh + 1) * 128, :], f.b, wB["s2"].b)
                for oc in range(8):
                    p.mm(pb[oc].h[:], f.h[:, oc * 128:(oc + 1) * 128], g1s[oh].h[:],
                         pb[oc].b, [f.b, g1s[oh].b], start=(oh == 0), stop=False)
            for oc in range(8):
                p.mm(pb[oc].h[:], sb2s.h[0:1, oc * 128:(oc + 1) * 128], c["ones_row512"].h[:],
                     pb[oc].b, [sb2s.b, c["ones_row512"].b], start=False, stop=True)
            for oc in range(8):
                p.rop("dve", "tensor_add", macc[oc].h[:], macc[oc].h[:], pb[oc].h[:],
                      outs=[macc[oc].b], reads=[macc[oc].b, pb[oc].b])
            for kc in range(8):
                p.rop("dve", "tensor_copy", ombf[kc].h[:], macc[kc].h[:], outs=[ombf[kc].b],
                      reads=[macc[kc].b])
            for kc in range(16):
                p.act(Gx[kc].h[:], x[kc].h[:], AF.Gelu, Gx[kc].b, [x[kc].b])
            for half in range(2):
                oc0 = half * 8
                csl = slice(oc0 * 128, (oc0 + 8) * 128)
                for kc in range(8):
                    f = wrot[kc % 4]
                    p.load(f.h[:, :1024], wB["uw"].ap[kc * 128:(kc + 1) * 128, csl], f.b,
                           wB["uw"].b)
                    for j in range(8):
                        p.mm(pb[j].h[:], f.h[:, j * 128:(j + 1) * 128], ombf[kc].h[:],
                             pb[j].b, [f.b, ombf[kc].b], start=(kc == 0), stop=False)
                for kc in range(16):
                    f = wrot[kc % 4]
                    p.load(f.h[:, :1024], wB["cw"].ap[kc * 128:(kc + 1) * 128, csl], f.b,
                           wB["cw"].b)
                    for j in range(8):
                        p.mm(pb[j].h[:], f.h[:, j * 128:(j + 1) * 128], Gx[kc].h[:],
                             pb[j].b, [f.b, Gx[kc].b], start=False, stop=False)
                for j in range(8):
                    oc = oc0 + j
                    p.mm(pb[j].h[:], ubr.h[0:1, oc * 128:(oc + 1) * 128], c["ones_row512"].h[:],
                         pb[j].b, [ubr.b, c["ones_row512"].b], start=False, stop=False)
                    p.mm(pb[j].h[:], cbr.h[0:1, oc * 128:(oc + 1) * 128], c["ones_row512"].h[:],
                         pb[j].b, [cbr.b, c["ones_row512"].b], start=False, stop=True)
                for j in range(8):
                    oc = oc0 + j
                    p.rop("dve", "tensor_add", x[oc].h[:], x[oc].h[:], pb[j].h[:],
                          outs=[x[oc].b], reads=[x[oc].b, pb[j].b])

            def putB(oc, bm, br, cs=cs):
                o = ev[oc % 2]
                p.rop("dve", "tensor_sub", o.h[:], x[oc].h[:], bm.h[:], outs=[o.b],
                      reads=[x[oc].b, bm.b])
                p.rop("dve", "tensor_mul", o.h[:], o.h[:], br.h[:], outs=[o.b], reads=[o.b, br.b])
                p.store(io["x2"].ap[oc * 128:(oc + 1) * 128, cs], o.h[:], o.b, io["x2"].b)
            ln_fm(p, c, x, (pb[0], pb[1], pb[2], pb[3]), sqb, rowt, rowt2, putB)
            ctx.phase()


def build_launch_B(debug=False):
    nc = bass.Bass("TRN2", target_bir_lowering=False, debug=False)
    p = P(nc)
    io = {}

    def inp(nm, shp):
        io[nm] = DT(nc.dram_tensor(nm, list(shp), F32, kind="ExternalInput").ap(), nm)
    inp("x1", [D, TCORE])
    inp("dw", [2048, 1024]); inp("uw", [1024, 2048]); inp("rw", [2048, 8])
    inp("s1", [1024, 1024]); inp("s2", [1024, 1024]); inp("cw", [2048, 2048])
    for i, hh in enumerate(HID):
        inp(f"e{i}w1", [1024, hh]); inp(f"e{i}w2", [hh, 1024])
        inp(f"e{i}b1f", [128, hh // 128]); inp(f"e{i}b2r", [1, 1024])
    inp("dbf", [128, 8]); inp("sb1f", [128, 8]); inp("sb2r", [1, 1024])
    inp("ubr", [1, 2048]); inp("cbr", [1, 2048]); inp("rbr", [1, 8])
    inp("negc", [1, 8]); inp("tbeps", [128, 8]); inp("idf", [128, 128])
    io["x2"] = DT(nc.dram_tensor("x2", [D, TCORE], F32, kind="ExternalOutput").ap(), "x2")
    with p.es:
        c = _mk_consts(p)
        emit_l0_ffn(p, c, io, debug=debug)
        p.ctx.finish()
        p.ctx.build(p.es)
    return nc


def prep_inputs_B(ins, x1):
    f32 = np.float32
    com = {
        "dw": np.asarray(ins["L0_ffn_down_w"], f32),
        "uw": np.asarray(ins["L0_ffn_up_w"], f32),
        "rw": np.asarray(ins["L0_ffn_router_w"], f32),
        "s1": np.asarray(ins["L0_ffn_shared_w1"], f32),
        "s2": np.asarray(ins["L0_ffn_shared_w2"], f32),
        "cw": np.asarray(ins["L0_ffn_core_w"], f32),
        "dbf": _fm(np.asarray(ins["L0_ffn_down_b"], f32)),
        "sb1f": _fm(np.asarray(ins["L0_ffn_shared_b1"], f32)),
        "sb2r": np.asarray(ins["L0_ffn_shared_b2"], f32).reshape(1, 1024),
        "ubr": np.asarray(ins["L0_ffn_up_b"], f32).reshape(1, 2048),
        "cbr": np.asarray(ins["L0_ffn_core_b"], f32).reshape(1, 2048),
        "rbr": np.asarray(ins["L0_ffn_router_b"], f32).reshape(1, 8),
        "negc": (-1e-7 * np.array([2 * 1024 * hh for hh in HID], f32)).reshape(1, 8),
        "tbeps": np.tile(((7 - np.arange(8)) * 1e-7).astype(f32), (128, 1)),
        "idf": np.eye(128, dtype=f32),
    }
    for i in range(8):
        com[f"e{i}w1"] = np.asarray(ins[f"L0_ffn_e{i}_w1"], f32)
        com[f"e{i}w2"] = np.asarray(ins[f"L0_ffn_e{i}_w2"], f32)
        com[f"e{i}b1f"] = _fm(np.asarray(ins[f"L0_ffn_e{i}_b1"], f32))
        com[f"e{i}b2r"] = np.asarray(ins[f"L0_ffn_e{i}_b2"], f32).reshape(1, 1024)
    xs = _slices_fm(x1)
    maps = []
    for cid in range(NCORE):
        m = dict(com)
        m["x1"] = xs[cid]
        maps.append(m)
    return maps


ASCALE = float(128.0 ** -0.5)


def emit_l1_attn(p, c, io, debug=False):
    ctx = p.ctx
    pb = [p.ps(f"C_pb{i}", [128, 512]) for i in range(7)]
    pbf = p.ps("C_pbf", [128, 1024], BF16)

    x2bfD = p.dram("C_x2bf", [D, TCORE], BF16)
    xAG = p.dram("C_xAG", [2, D, TCORE], BF16)
    ctxD = p.dram("C_ctx", [1024, 4096], BF16)
    outPD = p.dram("C_outP", [D, 4096], F32)
    outRD = p.dram("C_outR", [D, 4096], F32)

    band = p.sb("C_band", [128, 896])
    idb = p.sb("C_idb", [128, 128], BF16)
    sel0 = p.sb("C_sel0", [128, 1])
    sel1 = p.sb("C_sel1", [128, 1])
    for t, key in [(band, "Band"), (idb, "IDB"), (sel0, "sel0"), (sel1, "sel1")]:
        p.load(t.h[:], io[key].ap[:], t.b, io[key].b)

    with ExitStack() as ph:
        xt = [p.sb(f"C1_x{k}", [128, 512], F32, ph) for k in range(2)]
        xb = [p.sb(f"C1_xb{k}", [128, 512], BF16, ph) for k in range(2)]
        i = 0
        for kc in range(16):
            for tt in range(4):
                f, g = xt[i % 2], xb[i % 2]
                i += 1
                p.load(f.h[:], io["x2"].ap[kc * 128:(kc + 1) * 128, tt * 512:(tt + 1) * 512],
                       f.b, io["x2"].b)
                p.rop("dve", "tensor_copy", g.h[:], f.h[:], outs=[g.b], reads=[f.b])
                p.store(x2bfD.ap[kc * 128:(kc + 1) * 128, tt * 512:(tt + 1) * 512], g.h[:],
                        g.b, x2bfD.b)
        ctx.phase()
    p.ctx.op("pool", (lambda e, i_=x2bfD.ap.opt(), o_=xAG.ap.opt(): e.collective_compute(
        "AllGather", ALU.bypass, replica_groups=[[0, 1], [2, 3], [4, 5], [6, 7]],
        ins=[i_], outs=[o_])), reads=[x2bfD.b], writes=[xAG.b], cc=True)

    with ExitStack() as ph:
        wq = [p.sb(f"C2_wq{k}", [128, 512], BF16, ph) for k in range(16)]
        wk = [p.sb(f"C2_wk{k}", [128, 128], BF16, ph) for k in range(16)]
        wv = [p.sb(f"C2_wv{k}", [128, 128], BF16, ph) for k in range(16)]
        wtmp = [p.sb(f"C2_wt{k}", [128, 512], F32, ph) for k in range(2)]
        Qh = [p.sb(f"C2_Q{k}", [128, 4096], BF16, ph) for k in range(4)]
        Ku = p.sb("C2_K", [128, 4096], BF16, ph)
        Vu = p.sb("C2_V", [128, 4096], BF16, ph)
        VT = p.sb("C2_VT", [128, 4096], BF16, ph)
        xin = [p.sb(f"C2_xi{k}", [128, 512], BF16, ph) for k in range(2)]
        Et = [p.sb(f"C2_E{k}", [128, 512], BF16, ph) for k in range(2)]
        sm = p.sb("C2_sm", [128, 512], F32, ph)
        rcb = p.sb("C2_rcb", [128, 512], F32, ph)
        rrow = p.sb("C2_rr", [1, 512], F32, ph)
        cxe = [p.sb(f"C2_cx{k}", [128, 512], BF16, ph) for k in range(2)]
        for u in range(2):
            for kc in range(16):
                f = wtmp[kc % 2]
                p.load(f.h[:], io[f"wqu{u}"].ap[kc * 128:(kc + 1) * 128, :], f.b, io[f"wqu{u}"].b)
                p.rop("dve", "tensor_copy", wq[kc].h[:], f.h[:], outs=[wq[kc].b], reads=[f.b])
                p.load(f.h[:, :128], io[f"wku{u}"].ap[kc * 128:(kc + 1) * 128, :], f.b,
                       io[f"wku{u}"].b)
                p.rop("dve", "tensor_copy", wk[kc].h[:], f.h[:, :128], outs=[wk[kc].b],
                      reads=[f.b])
                p.load(f.h[:, :128], io[f"wvu{u}"].ap[kc * 128:(kc + 1) * 128, :], f.b,
                       io[f"wvu{u}"].b)
                p.rop("dve", "tensor_copy", wv[kc].h[:], f.h[:, :128], outs=[wv[kc].b],
                      reads=[f.b])
            for t8 in range(8):
                gts = slice(t8 * 512, (t8 + 1) * 512)
                r = t8 // 4
                lcs = slice((t8 % 4) * 512, (t8 % 4 + 1) * 512)
                for kc in range(16):
                    f = xin[kc % 2]
                    p.load(f.h[:], xAG.ap[r, kc * 128:(kc + 1) * 128, lcs], f.b, xAG.b)
                    for hd in range(4):
                        p.mm(pb[hd].h[:], wq[kc].h[:, hd * 128:(hd + 1) * 128], f.h[:],
                             pb[hd].b, [wq[kc].b, f.b], start=(kc == 0), stop=(kc == 15))
                    p.mm(pb[4].h[:], wk[kc].h[:], f.h[:], pb[4].b, [wk[kc].b, f.b],
                         start=(kc == 0), stop=(kc == 15))
                    p.mm(pb[5].h[:], wv[kc].h[:], f.h[:], pb[5].b, [wv[kc].b, f.b],
                         start=(kc == 0), stop=(kc == 15))
                for hd in range(4):
                    p.rop("dve", "tensor_copy", Qh[hd].h[:, gts], pb[hd].h[:],
                          outs=[Qh[hd].b], reads=[pb[hd].b])
                p.rop("dve", "tensor_copy", Ku.h[:, gts], pb[4].h[:], outs=[Ku.b],
                      reads=[pb[4].b])
                p.rop("dve", "tensor_copy", Vu.h[:, gts], pb[5].h[:], outs=[Vu.b],
                      reads=[pb[5].b])
            for kt in range(32):
                ks = slice(kt * 128, (kt + 1) * 128)
                p.ctx.op("pe", (lambda e, o=pbf.h[:, 0:128], i_=Vu.h[:, ks], id_=idb.h[:]:
                                e.transpose(o, i_, id_)),
                         reads=[Vu.b, idb.b], writes=[pbf.b])
                p.rop("dve", "tensor_copy", VT.h[:, ks], pbf.h[:, 0:128], outs=[VT.b],
                      reads=[pbf.b])
            for hd in range(4):
                for qt in range(8):
                    qs = slice(qt * 512, (qt + 1) * 512)
                    nkt = 4 * qt + 4
                    for kt in range(nkt):
                        ks = slice(kt * 128, (kt + 1) * 128)
                        p.mm(pb[0].h[:], Ku.h[:, ks], Qh[hd].h[:, qs], pb[0].b,
                             [Ku.b, Qh[hd].b], start=True, stop=True)
                        E = Et[kt % 2]
                        off = kt * 128 - qt * 512
                        if off >= 0:
                            p.rop("dve", "tensor_add", sm.h[:], pb[0].h[:],
                                  band.h[:, 384 - off:896 - off], outs=[sm.b],
                                  reads=[pb[0].b, band.b])
                            p.act(E.h[:], sm.h[:], AF.Exp, E.b, [sm.b, c["negC"].b],
                                  scale=ASCALE, bias=c["negC"].h[:])
                        else:
                            p.act(E.h[:], pb[0].h[:], AF.Exp, E.b, [pb[0].b, c["negC"].b],
                                  scale=ASCALE, bias=c["negC"].h[:])
                        p.mm(pb[1].h[0:1, :], c["ones_col_bf"].h[:], E.h[:], pb[1].b,
                             [E.b, c["ones_col_bf"].b], start=(kt == 0), stop=(kt == nkt - 1))
                        p.mm(pb[2].h[:], VT.h[:, ks], E.h[:], pb[2].b, [VT.b, E.b],
                             start=(kt == 0), stop=(kt == nkt - 1))
                    p.rop("dve", "reciprocal", rrow.h[0:1, :], pb[1].h[0:1, :],
                          outs=[rrow.b], reads=[pb[1].b])
                    p.mm(pb[3].h[:], c["ones_row_f"].h[:], rrow.h[0:1, :], pb[3].b,
                         [rrow.b, c["ones_row_f"].b], start=True, stop=True)
                    p.rop("dve", "tensor_copy", rcb.h[:], pb[3].h[:], outs=[rcb.b],
                          reads=[pb[3].b])
                    cx = cxe[qt % 2]
                    p.rop("dve", "tensor_mul", cx.h[:], pb[2].h[:], rcb.h[:],
                          outs=[cx.b], reads=[pb[2].b, rcb.b])
                    p.store(ctxD.ap[u * 512 + hd * 128:u * 512 + (hd + 1) * 128, qs],
                            cx.h[:], cx.b, ctxD.b)
        ctx.phase()

    with ExitStack() as ph:
        wd = [p.sb(f"C3_wd{k}", [128, 2048], BF16, ph) for k in range(8)]
        wtmp = [p.sb(f"C3_wt{k}", [128, 2048], F32, ph) for k in range(2)]
        cx = [p.sb(f"C3_cx{k}", [128, 512], BF16, ph) for k in range(8)]
        ev = [p.sb(f"C3_ev{k}", [128, 512], F32, ph) for k in range(2)]
        for u in range(2):
            for kc in range(4):
                f = wtmp[kc % 2]
                p.load(f.h[:], io[f"wdu{u}"].ap[kc * 128:(kc + 1) * 128, :], f.b,
                       io[f"wdu{u}"].b)
                p.rop("dve", "tensor_copy", wd[u * 4 + kc].h[:], f.h[:],
                      outs=[wd[u * 4 + kc].b], reads=[f.b])
        for t8 in range(8):
            gts = slice(t8 * 512, (t8 + 1) * 512)
            for kc in range(8):
                p.load(cx[kc].h[:], ctxD.ap[kc * 128:(kc + 1) * 128, gts], cx[kc].b, ctxD.b)
            for oc in range(16):
                bank = pb[oc % 2]
                for kc in range(8):
                    p.mm(bank.h[:], wd[kc].h[:, oc * 128:(oc + 1) * 128], cx[kc].h[:],
                         bank.b, [wd[kc].b, cx[kc].b], start=(kc == 0), stop=(kc == 7))
                o = ev[oc % 2]
                p.rop("act", "copy", o.h[:], bank.h[:], outs=[o.b], reads=[bank.b])
                p.store(outPD.ap[oc * 128:(oc + 1) * 128, gts], o.h[:], o.b, outPD.b)
        ctx.phase()
    p.ctx.op("pool", (lambda e, i_=outPD.ap.opt(), o_=outRD.ap.opt(): e.collective_compute(
        "AllReduce", ALU.add, replica_groups=[[0, 1], [2, 3], [4, 5], [6, 7]],
        ins=[i_], outs=[o_])), reads=[outPD.b], writes=[outRD.b], cc=True)

    with ExitStack() as ph:
        x = [p.sb(f"C5_x{k}", [128, 512], F32, ph) for k in range(16)]
        oa = [p.sb(f"C5_oa{k}", [128, 512], F32, ph) for k in range(2)]
        ob = [p.sb(f"C5_ob{k}", [128, 512], F32, ph) for k in range(2)]
        w1 = [p.sb(f"C5_w1{k}", [128, 512], F32, ph) for k in range(2)]
        sqb = [p.sb(f"C5_sq{k}", [128, 512], BF16, ph) for k in range(4)]
        rowt = p.sb("C5_row", [1, 512], F32, ph)
        rowt2 = p.sb("C5_row2", [1, 512], F32, ph)
        ev = [p.sb(f"C5_ev{k}", [128, 512], F32, ph) for k in range(2)]
        for tt in range(4):
            cs = slice(tt * 512, (tt + 1) * 512)
            for oc in range(16):
                rs = slice(oc * 128, (oc + 1) * 128)
                p.load(x[oc].h[:], io["x2"].ap[rs, cs], x[oc].b, io["x2"].b)
                a = oa[oc % 2]
                b_ = ob[oc % 2]
                w = w1[oc % 2]
                p.load(a.h[:], outRD.ap[rs, tt * 512:(tt + 1) * 512], a.b, outRD.b)
                p.load(b_.h[:], outRD.ap[rs, 2048 + tt * 512:2048 + (tt + 1) * 512],
                       b_.b, outRD.b)
                p.rop("dve", "tensor_scalar", w.h[:], a.h[:], sel0.h[:], None, ALU.mult,
                      outs=[w.b], reads=[a.b, sel0.b])
                p.rop("dve", "scalar_tensor_tensor", w.h[:], b_.h[:], sel1.h[:], w.h[:],
                      ALU.mult, ALU.add, outs=[w.b], reads=[b_.b, sel1.b, w.b])
                p.rop("dve", "tensor_add", x[oc].h[:], x[oc].h[:], w.h[:],
                      outs=[x[oc].b], reads=[x[oc].b, w.b])

            def putC(oc, bm, br, cs=cs):
                o = ev[oc % 2]
                p.rop("dve", "tensor_sub", o.h[:], x[oc].h[:], bm.h[:], outs=[o.b],
                      reads=[x[oc].b, bm.b])
                p.rop("dve", "tensor_mul", o.h[:], o.h[:], br.h[:], outs=[o.b],
                      reads=[o.b, br.b])
                p.store(io["x3"].ap[oc * 128:(oc + 1) * 128, cs], o.h[:], o.b, io["x3"].b)
            ln_fm(p, c, x, (pb[0], pb[1], pb[2], pb[3]), sqb, rowt, rowt2, putC)
        ctx.phase()


def build_launch_C(debug=False):
    nc = bass.Bass("TRN2", target_bir_lowering=False, debug=False)
    p = P(nc)
    io = {}

    def inp(nm, shp, dt=F32):
        io[nm] = DT(nc.dram_tensor(nm, list(shp), dt, kind="ExternalInput").ap(), nm)
    inp("x2", [D, TCORE])
    for u in range(2):
        inp(f"wqu{u}", [2048, 512]); inp(f"wku{u}", [2048, 128])
        inp(f"wvu{u}", [2048, 128]); inp(f"wdu{u}", [512, 2048])
    inp("Band", [128, 896]); inp("IDB", [128, 128], BF16)
    inp("sel0", [128, 1]); inp("sel1", [128, 1])
    io["x3"] = DT(nc.dram_tensor("x3", [D, TCORE], F32, kind="ExternalOutput").ap(), "x3")
    with p.es:
        c = _mk_consts(p)
        emit_l1_attn(p, c, io, debug=debug)
        p.ctx.finish()
        p.ctx.build(p.es)
    return nc


def prep_inputs_C(ins, x2):
    f32 = np.float32
    wq = np.asarray(ins["L1_attn_w_q"], f32)
    wkv = np.asarray(ins["L1_attn_w_kv"], f32)
    wd = np.asarray(ins["L1_attn_w_dense"], f32)
    band = np.zeros((128, 896), f32)
    for kk in range(128):
        for cc in range(896):
            if cc - kk < 384:
                band[kk, cc] = -1e30
    xs = _slices_fm(x2)
    maps = []
    for cid in range(NCORE):
        b, h = cid // 2, cid % 2
        m = {
            "x2": xs[cid],
            "Band": band,
            "IDB": np.eye(128, dtype=f32).astype(BF),
            "sel0": np.full((128, 1), 1.0 - h, f32),
            "sel1": np.full((128, 1), float(h), f32),
        }
        for u in range(2):
            g = 2 * h + u
            m[f"wqu{u}"] = np.ascontiguousarray(wq[:, g * 512:(g + 1) * 512])
            m[f"wku{u}"] = np.ascontiguousarray(wkv[:, g * 128:(g + 1) * 128])
            m[f"wvu{u}"] = np.ascontiguousarray(wkv[:, 512 + g * 128:512 + (g + 1) * 128])
            m[f"wdu{u}"] = np.ascontiguousarray(wd[g * 512:(g + 1) * 512, :])
        maps.append(m)
    return maps


def _dequant_decode(p, c, cf, vout, tmp):
    s, m4, r12 = tmp
    p.act(s.h[:], cf.h[:], AF.Sign, s.b, [cf.b, c["c75"].b], scale=-1.0, bias=c["c75"].h[:])
    p.rop("dve", "scalar_tensor_tensor", m4.h[:], s.h[:], 4.0, cf.h[:], ALU.mult, ALU.add,
          outs=[m4.b], reads=[s.b, cf.b])
    p.act(r12.h[:], m4.h[:], AF.Relu, r12.b, [m4.b, c["cm4"].b], scale=0.5, bias=c["cm4"].h[:])
    p.rop("dve", "scalar_tensor_tensor", r12.h[:], m4.h[:], 0.5, r12.h[:], ALU.mult, ALU.add,
          outs=[r12.b], reads=[m4.b, r12.b])
    p.act(vout.h[:], m4.h[:], AF.Relu, vout.b, [m4.b, c["cm10"].b], bias=c["cm10"].h[:])
    p.rop("dve", "tensor_add", r12.h[:], r12.h[:], vout.h[:], outs=[r12.b], reads=[r12.b, vout.b])
    p.rop("dve", "tensor_scalar_add", r12.h[:], r12.h[:], -2.0, outs=[r12.b], reads=[r12.b])
    p.rop("dve", "tensor_mul", vout.h[:], s.h[:], r12.h[:], outs=[vout.b], reads=[s.b, r12.b])


def emit_l1_ffn_dequant(p, c, io, debug=False):
    ctx = p.ctx
    pb = [p.ps(f"D0_pb{i}", [128, 512]) for i in range(2)]

    gshD = p.dram("D_gsh", [2048, 2048], BF16)
    ushD = p.dram("D_ush", [2048, 2048], BF16)
    dshD = p.dram("D_dsh", [16384, 256], BF16)
    io["gAG"] = p.dram("D_gAG", [8, 2048, 2048], BF16)
    io["uAG"] = p.dram("D_uAG", [8, 2048, 2048], BF16)
    io["dAG"] = p.dram("D_dAG", [8, 16384, 256], BF16)

    sel64 = p.sb("D_sel", [64, 2048])
    idf = p.sb("D0_idf", [128, 128])
    idb = p.sb("D0_idb", [128, 128], BF16)
    for t, key in [(sel64, "sel64"), (idf, "idf"), (idb, "IDB")]:
        p.load(t.h[:], io[key].ap[:], t.b, io[key].b)

    with ExitStack() as ph:
        pbf = T(ph.enter_context(p.nc.psum_tensor("D0_pbf", [128, 1024], BF16)), "D0_pbf")
        pk = p.sb("D0_pk", [128, 1024], I32, ph)
        ii = p.sb("D0_ii", [128, 1024], I32, ph)
        cf = [p.sb(f"D0_cf{k}", [128, 1024], F32, ph) for k in range(2)]
        vv = [p.sb(f"D0_v{k}", [128, 1024], F32, ph) for k in range(2)]
        tmp = [p.sb(f"D0_t{k}", [128, 1024], F32, ph) for k in range(3)]
        wrow = p.sb("D0_wr", [128, 2048], BF16, ph)
        tb = p.sb("D0_tb", [128, 128], BF16, ph)
        wT = [p.sb(f"D0_wT{k}", [128, 2048], BF16, ph) for k in range(16)]
        ssc = p.sb("D0_ssc", [128, 64], F32, ph)
        sT = p.sb("D0_sT", [64, 2048], F32, ph)
        dscT = p.sb("D0_dT", [64, 256], F32, ph)
        sout = p.sb("D0_so", [128, 512], BF16, ph)

        def dq_tile(pksrc, r0, wcol, nwords):
            p.load(pk.h[:, :nwords], pksrc.ap[r0:r0 + 128, wcol:wcol + nwords], pk.b, pksrc.b)
            p.rop("dve", "tensor_single_scalar", ii.h[:, :nwords], pk.h[:, :nwords], 15,
                  ALU.bitwise_and, outs=[ii.b], reads=[pk.b])
            p.rop("dve", "tensor_copy", cf[0].h[:, :nwords], ii.h[:, :nwords],
                  outs=[cf[0].b], reads=[ii.b])
            p.rop("dve", "tensor_single_scalar", ii.h[:, :nwords], pk.h[:, :nwords], 4,
                  ALU.logical_shift_right, outs=[ii.b], reads=[pk.b])
            p.rop("dve", "tensor_copy", cf[1].h[:, :nwords], ii.h[:, :nwords],
                  outs=[cf[1].b], reads=[ii.b])
            _dequant_decode(p, c, cf[0], vv[0], tmp)
            _dequant_decode(p, c, cf[1], vv[1], tmp)
            ap_even = bass.AP(wrow.h, 0, [[2048, 128], [2, nwords]])
            ap_odd = bass.AP(wrow.h, 1, [[2048, 128], [2, nwords]])
            p.rop("dve", "tensor_copy", ap_even, vv[0].h[:, :nwords], outs=[wrow.b],
                  reads=[vv[0].b])
            p.rop("dve", "tensor_copy", ap_odd, vv[1].h[:, :nwords], outs=[wrow.b],
                  reads=[vv[1].b])

        for wname, pkk, sck, shD in [("g", "gpk", "gsc", gshD), ("u", "upk", "usc", ushD)]:
            for nb in range(16):
                dq_tile(io[pkk], nb * 128, 0, 1024)
                for kc in range(16):
                    p.ctx.op("pe", (lambda e, o=pbf.h[:, 0:128], i_=wrow.h[:, kc * 128:(kc + 1) * 128],
                                    id_=idb.h[:]: e.transpose(o, i_, id_)),
                             reads=[wrow.b, idb.b], writes=[pbf.b])
                    p.rop("dve", "tensor_copy", wT[kc].h[:, nb * 128:(nb + 1) * 128],
                          pbf.h[:, 0:128], outs=[wT[kc].b], reads=[pbf.b])
            for nb in range(16):
                p.load(ssc.h[:], io[sck].ap[nb * 128:(nb + 1) * 128, :], ssc.b, io[sck].b)
                p.ctx.op("pe", (lambda e, o=pb[0].h[0:64, 0:128], i_=ssc.h[:], id_=idf.h[:]:
                                e.transpose(o, i_, id_)), reads=[ssc.b, idf.b], writes=[pb[0].b])
                p.rop("dve", "tensor_copy", sT.h[:, nb * 128:(nb + 1) * 128], pb[0].h[0:64, 0:128],
                      outs=[sT.b], reads=[pb[0].b])
            for kc in range(16):
                for q4 in range(4):
                    fs = slice(q4 * 512, (q4 + 1) * 512)
                    p.mm(pb[1].h[:], sel64.h[:, kc * 128:(kc + 1) * 128], sT.h[:, fs],
                         pb[1].b, [sel64.b, sT.b], start=True, stop=True)
                    p.rop("dve", "tensor_mul", sout.h[:], wT[kc].h[:, fs], pb[1].h[:],
                          outs=[sout.b], reads=[wT[kc].b, pb[1].b])
                    p.store(shD.ap[kc * 128:(kc + 1) * 128, fs], sout.h[:], sout.b, shD.b)
        for ht in range(2):
            for wc in range(8):
                p.load(ssc.h[:, :64], io["dsc"].ap[ht * 128:(ht + 1) * 128,
                                                   wc * 64:(wc + 1) * 64], ssc.b, io["dsc"].b)
                p.ctx.op("pe", (lambda e, o=pb[0].h[0:64, 0:128], i_=ssc.h[:, 0:64],
                                id_=idf.h[:]: e.transpose(o, i_, id_)),
                         reads=[ssc.b, idf.b], writes=[pb[0].b])
                p.rop("dve", "tensor_copy", dscT.h[:, 0:128], pb[0].h[0:64, 0:128],
                      outs=[dscT.b], reads=[pb[0].b])
                dq_tile(io["dpk"], ht * 128, wc * 1024, 1024)
                for j in range(16):
                    p.ctx.op("pe", (lambda e, o=pbf.h[:, 0:128],
                                    i_=wrow.h[:, j * 128:(j + 1) * 128],
                                    id_=idb.h[:]: e.transpose(o, i_, id_)),
                             reads=[wrow.b, idb.b], writes=[pbf.b])
                    p.rop("dve", "tensor_copy", tb.h[:], pbf.h[:, 0:128], outs=[tb.b],
                          reads=[pbf.b])
                    p.mm(pb[1].h[:, 0:128], sel64.h[:, j * 128:(j + 1) * 128],
                         dscT.h[:, 0:128], pb[1].b, [sel64.b, dscT.b], start=True, stop=True)
                    p.rop("dve", "tensor_mul", sout.h[:, 0:128], tb.h[:], pb[1].h[:, 0:128],
                          outs=[sout.b], reads=[tb.b, pb[1].b])
                    p.store(dshD.ap[wc * 2048 + j * 128:wc * 2048 + (j + 1) * 128,
                                    ht * 128:(ht + 1) * 128],
                            sout.h[:, 0:128], sout.b, dshD.b)
        ctx.phase(exclude=("cc",))
    for src_, dst_ in [(gshD, io["gAG"]), (ushD, io["uAG"]), (dshD, io["dAG"])]:
        p.ctx.op("pool", (lambda e, i_=src_.ap.opt(), o_=dst_.ap.opt(): e.collective_compute(
            "AllGather", ALU.bypass, replica_groups=[[0, 1, 2, 3, 4, 5, 6, 7]],
            ins=[i_], outs=[o_])), reads=[src_.b], writes=[dst_.b], cc=True)


def emit_l1_ffn_main(p, c, io, debug=False):
    ctx = p.ctx
    gAG, uAG, dAG = io["gAG"], io["uAG"], io["dAG"]
    pb = [p.ps(f"D_pb{i}", [128, 512]) for i in range(8)]
    wdT = p.dram("D_wdT", [16384, 2048], BF16)
    with ExitStack() as ph:
        rp = [p.sb(f"Drp{k}", [128, 2048], BF16, ph) for k in range(4)]
        for ncg in range(128):
            f = rp[ncg % 4]
            for r_ in range(8):
                p.load(f.h[:, r_ * 256:(r_ + 1) * 256],
                       dAG.ap[r_, ncg * 128:(ncg + 1) * 128, 0:256], f.b, dAG.b)
            p.store(wdT.ap[ncg * 128:(ncg + 1) * 128, :], f.h[:], f.b, wdT.b)
        ctx.phase(exclude=("cc",))
    with ExitStack() as ph:
        x = [p.sb(f"D1_x{k}", [128, 512], F32, ph) for k in range(16)]
        xbf = [p.sb(f"D1_xb{k}", [128, 512], BF16, ph) for k in range(16)]
        gate = [p.sb(f"D1_g{k}", [128, 512], BF16, ph) for k in range(16)]
        inter = [p.sb(f"D1_i{k}", [128, 512], BF16, ph) for k in range(16)]
        acc = [p.sb(f"D1_a{k}", [128, 512], F32, ph) for k in range(16)]
        wpool = [p.sb(f"D1_wp{k}", [128, 1024], BF16, ph) for k in range(16)]
        gtmp = [p.sb(f"D1_gt{k}", [128, 512], F32, ph) for k in range(2)]
        sqb = [p.sb(f"D1_sq{k}", [128, 512], BF16, ph) for k in range(4)]
        rowt = p.sb("D1_row", [1, 512], F32, ph)
        rowt2 = p.sb("D1_row2", [1, 512], F32, ph)
        ev = [p.sb(f"D1_ev{k}", [128, 512], F32, ph) for k in range(2)]
        wi_ = [0]

        def wnext():
            wi_[0] += 1
            return wpool[wi_[0] % 16]
        for tt in range(4):
            cs = slice(tt * 512, (tt + 1) * 512)
            for kc in range(16):
                p.load(x[kc].h[:], io["x3"].ap[kc * 128:(kc + 1) * 128, cs], x[kc].b,
                       io["x3"].b)
                p.rop("dve", "tensor_copy", xbf[kc].h[:], x[kc].h[:], outs=[xbf[kc].b],
                      reads=[x[kc].b])
            for g in range(8):
                for AGW, is_up in [(gAG, False), (uAG, True)]:
                    for sg in range(2):
                        for kc in range(16):
                            f = wnext()
                            p.load(f.h[:], AGW.ap[g, kc * 128:(kc + 1) * 128,
                                                  sg * 1024:(sg + 1) * 1024], f.b, AGW.b)
                            for j in range(8):
                                p.mm(pb[j].h[:], f.h[:, j * 128:(j + 1) * 128], xbf[kc].h[:],
                                     pb[j].b, [f.b, xbf[kc].b], start=(kc == 0), stop=(kc == 15))
                        for j in range(8):
                            nc_ = sg * 8 + j
                            if not is_up:
                                p.rop("act", "copy", gate[nc_].h[:], pb[j].h[:],
                                      outs=[gate[nc_].b], reads=[pb[j].b])
                            else:
                                ga = gtmp[j % 2]
                                p.act(ga.h[:], gate[nc_].h[:], AF.Gelu_apprx_tanh, ga.b,
                                      [gate[nc_].b])
                                p.rop("dve", "tensor_mul", inter[nc_].h[:], ga.h[:], pb[j].h[:],
                                      outs=[inter[nc_].b], reads=[ga.b, pb[j].b])
                for half in range(2):
                    for nc_ in range(16):
                        f = wnext()
                        p.load(f.h[:], wdT.ap[g * 2048 + nc_ * 128:g * 2048 + (nc_ + 1) * 128,
                                              half * 1024:(half + 1) * 1024], f.b, wdT.b)
                        for j in range(8):
                            p.mm(pb[j].h[:], f.h[:, j * 128:(j + 1) * 128], inter[nc_].h[:],
                                 pb[j].b, [f.b, inter[nc_].b], start=(nc_ == 0), stop=(nc_ == 15))
                    for j in range(8):
                        oc = half * 8 + j
                        if g == 0:
                            p.rop("dve", "tensor_copy", acc[oc].h[:], pb[j].h[:],
                                  outs=[acc[oc].b], reads=[pb[j].b])
                        else:
                            p.rop("dve", "tensor_add", acc[oc].h[:], acc[oc].h[:], pb[j].h[:],
                                  outs=[acc[oc].b], reads=[acc[oc].b, pb[j].b])
            for oc in range(16):
                p.rop("dve", "tensor_add", x[oc].h[:], x[oc].h[:], acc[oc].h[:],
                      outs=[x[oc].b], reads=[x[oc].b, acc[oc].b])

            def putD(oc, bm, br, cs=cs):
                o = ev[oc % 2]
                p.rop("dve", "tensor_sub", o.h[:], x[oc].h[:], bm.h[:], outs=[o.b],
                      reads=[x[oc].b, bm.b])
                p.rop("dve", "tensor_mul", o.h[:], o.h[:], br.h[:], outs=[o.b],
                      reads=[o.b, br.b])
                p.store(io["out"].ap[oc * 128:(oc + 1) * 128, cs], o.h[:], o.b, io["out"].b)
            ln_fm(p, c, x, (pb[0], pb[1], pb[2], pb[3]), sqb, rowt, rowt2, putD)
            ctx.phase()


def emit_l1_ffn(p, c, io, debug=False):
    emit_l1_ffn_dequant(p, c, io, debug)
    emit_l1_ffn_main(p, c, io, debug)


def build_launch_D(debug=False):
    nc = bass.Bass("TRN2", target_bir_lowering=False, debug=False)
    p = P(nc)
    io = {}

    def inp(nm, shp, dt=F32):
        io[nm] = DT(nc.dram_tensor(nm, list(shp), dt, kind="ExternalInput").ap(), nm)
    inp("x3", [D, TCORE])
    inp("gpk", [2048, 1024], I32); inp("gsc", [2048, 64])
    inp("upk", [2048, 1024], I32); inp("usc", [2048, 64])
    inp("dpk", [256, 8192], I32); inp("dsc", [256, 512])
    inp("sel64", [64, 2048]); inp("idf", [128, 128]); inp("IDB", [128, 128], BF16)
    io["out"] = DT(nc.dram_tensor("out", [D, TCORE], F32, kind="ExternalOutput").ap(), "out")
    with p.es:
        c = _mk_consts(p)
        emit_l1_ffn(p, c, io, debug=debug)
        p.ctx.finish()
        p.ctx.build(p.es)
    return nc


def prep_inputs_D(ins, x3):
    f32 = np.float32
    sel64 = np.zeros((64, 2048), f32)
    for kc in range(16):
        for pp in range(128):
            sel64[kc * 4 + pp // 32, kc * 128 + pp] = 1.0
    com = {
        "sel64": sel64,
        "idf": np.eye(128, dtype=f32),
        "IDB": np.eye(128, dtype=f32).astype(BF),
    }
    gp = np.asarray(ins["L1_ffn_gate_W_packed"], np.int32)
    gs = np.asarray(ins["L1_ffn_gate_scales"], f32)
    up = np.asarray(ins["L1_ffn_up_W_packed"], np.int32)
    us = np.asarray(ins["L1_ffn_up_scales"], f32)
    dp = np.asarray(ins["L1_ffn_down_W_packed"], np.int32)
    ds = np.asarray(ins["L1_ffn_down_scales"], f32)
    xs = _slices_fm(x3)
    maps = []
    for cid in range(NCORE):
        b, h = cid // 2, cid % 2
        m = dict(com)
        m["x3"] = xs[cid]
        m["gpk"] = np.ascontiguousarray(gp[cid * 2048:(cid + 1) * 2048])
        m["gsc"] = np.ascontiguousarray(gs[cid * 2048:(cid + 1) * 2048])
        m["upk"] = np.ascontiguousarray(up[cid * 2048:(cid + 1) * 2048])
        m["usc"] = np.ascontiguousarray(us[cid * 2048:(cid + 1) * 2048])
        m["dpk"] = np.ascontiguousarray(dp[cid * 256:(cid + 1) * 256])
        m["dsc"] = np.ascontiguousarray(ds[cid * 256:(cid + 1) * 256])
        maps.append(m)
    return maps


def build_fused(debug=False):
    nc = bass.Bass("TRN2", target_bir_lowering=False, debug=False)
    p = P(nc)
    io = {}

    def inp(nm, shp, dt=F32):
        io[nm] = DT(nc.dram_tensor(nm, list(shp), dt, kind="ExternalInput").ap(), nm)
    inp("x", [D, TCORE])
    for nm, shp in [("wa", [D, D]), ("wi", [D, D]), ("wv", [D, D]), ("wo", [D, D]),
                    ("wgu", [D, 16384]), ("wdn", [8192, D])]:
        inp(nm, shp)
    for nm in ["prew", "dcb", "ffnw"]:
        inp(nm, [128, 16])
    inp("wsel", [128, 1])
    inp("dw", [2048, 1024]); inp("uw", [1024, 2048]); inp("rw", [2048, 8])
    inp("s1", [1024, 1024]); inp("s2", [1024, 1024]); inp("cw", [2048, 2048])
    for i, hh in enumerate(HID):
        inp(f"e{i}w1", [1024, hh]); inp(f"e{i}w2", [hh, 1024])
        inp(f"e{i}b1f", [128, hh // 128]); inp(f"e{i}b2r", [1, 1024])
    inp("dbf", [128, 8]); inp("sb1f", [128, 8]); inp("sb2r", [1, 1024])
    inp("ubr", [1, 2048]); inp("cbr", [1, 2048]); inp("rbr", [1, 8])
    inp("negc", [1, 8]); inp("tbeps", [128, 8]); inp("idf", [128, 128])
    for u in range(2):
        inp(f"wqu{u}", [2048, 512]); inp(f"wku{u}", [2048, 128])
        inp(f"wvu{u}", [2048, 128]); inp(f"wdu{u}", [512, 2048])
    inp("Band", [128, 896]); inp("IDB", [128, 128], BF16)
    inp("sel0", [128, 1]); inp("sel1", [128, 1])
    inp("gpk", [2048, 1024], I32); inp("gsc", [2048, 64])
    inp("upk", [2048, 1024], I32); inp("usc", [2048, 64])
    inp("dpk", [256, 8192], I32); inp("dsc", [256, 512])
    inp("sel64", [64, 2048])
    io["x1"] = p.dram("x1", [D, TCORE])
    io["x2"] = p.dram("x2", [D, TCORE])
    io["x3"] = p.dram("x3", [D, TCORE])
    io["out"] = DT(nc.dram_tensor("out", [D, TCORE], F32, kind="ExternalOutput").ap(), "out")
    with p.es:
        c = _mk_consts(p)
        for tag, emitter in [("blkD0", emit_l1_ffn_dequant), ("blkA", emit_l0_attn),
                             ("blkB", emit_l0_ffn), ("blkC", emit_l1_attn),
                             ("blkD", emit_l1_ffn_main)]:
            p.ctx.tag = tag
            with p.scope():
                emitter(p, c, io, debug=False)
        p.ctx.tag = None
        p.ctx.finish()
        p.ctx.build(p.es)
    return nc


def prep_fused(ins):
    mA = prep_inputs_A(ins)
    mB = prep_inputs_B(ins, [None] * NCORE)
    mC = prep_inputs_C(ins, [None] * NCORE)
    mD = prep_inputs_D(ins, [None] * NCORE)
    maps = []
    for cid in range(NCORE):
        m = {}
        for mm in (mA[cid], mB[cid], mC[cid], mD[cid]):
            for k, v in mm.items():
                if v is not None:
                    m[k] = v
        maps.append(m)
    return maps


def kernel(**inputs):
    cores = list(range(NCORE))
    res = run_bass_kernel_spmd(_nc_for("F", build_fused), prep_fused(inputs), core_ids=cores)
    out = np.empty((4, 4096, 2048), np.float32)
    for cid in cores:
        b, h = cid // 2, cid % 2
        out[b, h * 2048:(h + 1) * 2048, :] = res.results[cid]["out"].T
    return out


# version 11 (final)
# speedup vs baseline: 1.0101x; 1.0101x over previous
import numpy as np
import ml_dtypes
import concourse.bass as bass
import concourse.mybir as mybir
from concourse.bass_utils import run_bass_kernel_spmd
from contextlib import ExitStack

F32 = mybir.dt.float32
BF16 = mybir.dt.bfloat16
I32 = mybir.dt.int32
AF = mybir.ActivationFunctionType
ALU = mybir.AluOpType
AX = mybir.AxisListType
BF = ml_dtypes.bfloat16

NCORE = 8
D = 2048
TCORE = 2048

_ENG_ATTR = {"sp": "sync", "pe": "tensor", "act": "scalar", "dve": "vector", "pool": "gpsimd"}
_STREAMS = ("pe", "act", "dve", "pool", "sp", "spd", "actd", "poold", "cc")


class Buf:
    __slots__ = ("name", "writers", "readers")

    def __init__(self, name):
        self.name = name
        self.writers = []
        self.readers = []


class Rec:
    __slots__ = ("eng", "stream", "fn", "inc", "sidx", "need_inc", "waits", "val", "tag")

    def __init__(self, eng, stream, fn, inc, sidx):
        self.eng = eng
        self.stream = stream
        self.fn = fn
        self.inc = inc
        self.sidx = sidx
        self.need_inc = False
        self.waits = ()
        self.val = 0
        self.tag = None


class T:
    __slots__ = ("h", "b")

    def __init__(self, h, name):
        self.h = h
        self.b = Buf(name)


class DT:
    __slots__ = ("ap", "b")

    def __init__(self, ap, name):
        self.ap = ap
        self.b = Buf(name)


_QCFG = {"spd": (16, 16), "actd": (8, 16), "poold": (16, 16), "cc": (4, 1)}
_ECHO_LAG = 14
_ECHO_BATCH = 8


class Ctx:
    def __init__(self, nc):
        self.nc = nc
        self.per_eng = {e: [] for e in _ENG_ATTR}
        self.per_stream = {s: [] for s in _STREAMS}
        self.maxwait = {}
        self.tag = None

    def op(self, eng, fn, reads=(), writes=(), dma=False, cc=False):
        stream = "cc" if cc else (eng + "d" if dma else eng)
        is_async = stream in _QCFG
        inc = _QCFG[stream][1] if is_async else 1
        rec = Rec(eng, stream, fn, inc, len(self.per_stream[stream]))
        if is_async:
            rec.need_inc = True
        pend = {}

        def want(p):
            if (not is_async) and p.stream == eng == "pe":
                return
            if self.maxwait.get((eng, p.stream), -1) >= p.sidx:
                return
            if pend.get(p.stream, (-1, None))[0] < p.sidx:
                pend[p.stream] = (p.sidx, p)

        for b in reads:
            for p in b.writers:
                want(p)
        for b in writes:
            for p in b.writers:
                want(p)
            for p in b.readers:
                want(p)
        waits = []
        for s, (sidx, p) in pend.items():
            self.maxwait[(eng, s)] = sidx
            p.need_inc = True
            waits.append(p)
        rec.waits = tuple(waits)
        rec.tag = self.tag
        self.per_eng[eng].append(rec)
        self.per_stream[stream].append(rec)
        for b in reads:
            b.readers.append(rec)
        for b in writes:
            b.writers = [rec]
            b.readers = []
        return rec

    def _barrier_waits(self, eng, exclude=()):
        waits = []
        for s in _STREAMS:
            if s in exclude or not self.per_stream[s]:
                continue
            p = self.per_stream[s][-1]
            if s not in _QCFG and s == eng:
                continue
            if self.maxwait.get((eng, s), -1) >= p.sidx:
                continue
            self.maxwait[(eng, s)] = p.sidx
            p.need_inc = True
            waits.append(p)
        return waits

    def phase(self, exclude=()):
        for eng in _ENG_ATTR:
            waits = self._barrier_waits(eng, exclude)
            if waits:
                rec = Rec(eng, eng, lambda e: e.nop(), 1, len(self.per_stream[eng]))
                rec.waits = tuple(waits)
                self.per_eng[eng].append(rec)
                self.per_stream[eng].append(rec)

    def finish(self):
        waits = self._barrier_waits("pool")
        rec = Rec("pool", "pool", lambda e: e.nop(), 1, len(self.per_stream["pool"]))
        rec.waits = tuple(waits)
        self.per_eng["pool"].append(rec)
        self.per_stream["pool"].append(rec)

    def build(self, es):
        nc = self.nc
        csem = {s: es.enter_context(nc.semaphore("sem_" + s))
                for s in _STREAMS if s not in _QCFG}
        raw = {}
        ords = {}
        for q, (k, _inc) in _QCFG.items():
            if not self.per_stream[q]:
                continue
            raw[q] = [es.enter_context(nc.semaphore(f"raw_{q}{i}")) for i in range(k)]
            ords[q] = es.enter_context(nc.semaphore(f"ord_{q}"))
        for s, recs in self.per_stream.items():
            if s in _QCFG:
                k, inc = _QCFG[s]
                for i, r in enumerate(recs):
                    r.val = inc * (i // k + 1)
            else:
                v = 0
                for r in recs:
                    if r.need_inc:
                        v += r.inc
                    r.val = v
        block = es.enter_context(nc.Block())
        for eng, attr in _ENG_ATTR.items():
            recs = self.per_eng[eng]
            if not recs:
                continue

            def body(proxy, recs=recs, eng=eng):
                pending = {q: [] for q in _QCFG}
                cur_tag = [None, None]

                def set_tag(t):
                    if t == cur_tag[0]:
                        return
                    if cur_tag[0] is not None:
                        nc.leave_named_scope(cur_tag[0], cur_tag[1], False)
                    if t is not None:
                        sid, _ = nc.enter_named_scope(t, False)
                        cur_tag[0], cur_tag[1] = t, sid
                    else:
                        cur_tag[0], cur_tag[1] = None, None

                def emit_echo_n(pq, n):
                    if n <= 0:
                        return
                    for _ in range(n):
                        r = pq.pop(0)
                        k = _QCFG[r.stream][0]
                        proxy.wait_ge(raw[r.stream][r.sidx % k], r.val)
                    proxy.sem_inc(ords[r.stream], n)

                for r in recs:
                    set_tag(r.tag)
                    if r.waits:
                        for q, pq in pending.items():
                            emit_echo_n(pq, len(pq))
                    for p in r.waits:
                        if p.stream in _QCFG:
                            proxy.wait_ge(ords[p.stream], p.sidx + 1)
                        else:
                            proxy.wait_ge(csem[p.stream], p.val)
                    if r.stream in _QCFG:
                        pq = pending[r.stream]
                        k = _QCFG[r.stream][0]
                        inst = r.fn(proxy)
                        inst.then_inc(raw[r.stream][r.sidx % k], r.inc)
                        pq.append(r)
                        if len(pq) > _ECHO_LAG:
                            emit_echo_n(pq, _ECHO_BATCH)
                    else:
                        inst = r.fn(proxy)
                        if r.need_inc:
                            inst.then_inc(csem[r.stream], r.inc)
                for q, pq in pending.items():
                    emit_echo_n(pq, len(pq))
                set_tag(None)
            getattr(block, attr)(body)


class P:

    def __init__(self, nc):
        self.nc = nc
        self.ctx = Ctx(nc)
        self.es = ExitStack()
        self.stacks = [self.es]
        self.n_sb = 0

    def scope(self):
        import contextlib

        @contextlib.contextmanager
        def _s():
            st = ExitStack()
            self.stacks.append(st)
            try:
                with st:
                    yield st
            finally:
                self.stacks.pop()
        return _s()

    def sb(self, name, shape, dt=F32, stack=None):
        h = (stack or self.stacks[-1]).enter_context(self.nc.sbuf_tensor(name, list(shape), dt))
        return T(h, name)

    def ps(self, name, shape, dt=F32):
        h = self.stacks[-1].enter_context(self.nc.psum_tensor(name, list(shape), dt))
        return T(h, name)

    def dram(self, name, shape, dt=F32, kind="Internal"):
        ap = self.nc.dram_tensor(name, list(shape), dt, kind=kind).ap()
        return DT(ap, name)

    _ldq = 0

    def rop(self, eng, method, *args, outs=(), reads=(), dma=False, cc=False, **kw):
        def fn(e, m=method, a=args, k=dict(kw)):
            return getattr(e, m)(*a, **k)
        return self.ctx.op(eng, fn, reads=list(reads), writes=list(outs), dma=dma, cc=cc)

    def load(self, dst_ap, src_ap, dstb, srcb=None, eng=None):
        if eng is None:
            self._ldq += 1
            eng = "sp" if self._ldq % 2 else "act"
        rd = [srcb] if srcb is not None else []
        self.ctx.op(eng, lambda e: e.dma_start(out=dst_ap, in_=src_ap),
                    reads=rd, writes=[dstb], dma=True)

    def store(self, dst_ap, src_ap, srcb, dstb, eng="pool"):
        self.ctx.op(eng, lambda e: e.dma_start(out=dst_ap, in_=src_ap),
                    reads=[srcb], writes=[dstb], dma=True)

    def mm(self, out_ap, lhs_ap, rhs_ap, outb, rbufs, start, stop):
        self.ctx.op("pe", lambda e: e.matmul(out_ap, lhs_ap, rhs_ap, start=start, stop=stop),
                    reads=list(rbufs), writes=[outb])

    def act(self, out_ap, in_ap, func, outb, rbufs, bias=None, scale=None, accum=None, accumb=None):
        kw = {}
        if bias is not None:
            kw["bias"] = bias
        if scale is not None:
            kw["scale"] = scale
        if accum is not None:
            kw["accum_out"] = accum
        wr = [outb] + ([accumb] if accumb is not None else [])
        self.ctx.op("act", lambda e: e.activation(out=out_ap, in_=in_ap, func=func, **kw),
                    reads=list(rbufs), writes=wr)

    def dve(self, fn, outb, rbufs):
        self.ctx.op("dve", fn, reads=list(rbufs), writes=[outb])

    def pool(self, fn, outb, rbufs=()):
        self.ctx.op("pool", fn, reads=list(rbufs), writes=[outb])


def _mk_consts(p):
    c = {}
    c["ones_col_bf"] = p.sb("c_ones_col_bf", [128, 1], BF16)
    c["ones_row_f"] = p.sb("c_ones_row_f", [1, 128], F32)
    c["ones64"] = p.sb("c_ones64", [128, 64], F32)
    c["zero32"] = p.sb("c_zero32", [128, 32], F32)
    c["ones_row_bf"] = p.sb("c_ones_row_bf", [1, 128], BF16)
    c["eps6"] = p.sb("c_eps6", [128, 1])
    c["eps5"] = p.sb("c_eps5", [128, 1])
    c["negC"] = p.sb("c_negC", [128, 1])
    c["c75"] = p.sb("c_c75", [128, 1])
    c["cm4"] = p.sb("c_cm4", [128, 1])
    c["cm10"] = p.sb("c_cm10", [128, 1])
    vals = {"zero32": 0.0, "eps6": 1e-6, "eps5": 1e-5, "negC": -12.0,
            "c75": 7.5, "cm4": -4.0, "cm10": -10.0}
    for k in c:
        val = vals.get(k, 1.0)
        t = c[k]
        p.pool(lambda e, t=t, val=val: e.memset(t.h[:], val), t.b)
    return c


def rmsnorm_fm(p, c, xt, w_tb, wcol, out_put, banks, t_w, sq_bufs, tmp_row):
    stat, bcast = banks
    for kc in range(16):
        tsrc, ap = xt[kc]
        sq = sq_bufs[kc % 2]
        p.act(sq.h[:, :t_w], ap, AF.Square, sq.b, [tsrc.b])
        p.mm(stat.h[0:1, :t_w], c["ones_col_bf"].h[:], sq.h[:, :t_w], stat.b, [sq.b, c["ones_col_bf"].b],
             start=(kc == 0), stop=(kc == 15))
    p.act(tmp_row.h[0:1, :t_w], stat.h[0:1, :t_w], AF.Sqrt, tmp_row.b, [stat.b, c["eps6"].b],
          scale=1.0 / 2048.0, bias=c["eps6"].h[0:1, :])
    p.dve(lambda e: e.reciprocal(tmp_row.h[0:1, :t_w], tmp_row.h[0:1, :t_w]), tmp_row.b, [tmp_row.b])
    p.mm(bcast.h[:, :t_w], c["ones_row_f"].h[:], tmp_row.h[0:1, :t_w], bcast.b,
         [tmp_row.b, c["ones_row_f"].b], start=True, stop=True)
    for kc in range(16):
        tsrc, ap = xt[kc]
        out_put(kc, ap, tsrc, bcast)


def emit_l0_attn(p, c, io, debug=False):
    nc, ctx = p.nc, p.ctx
    b0, b1, b2, b3, b4, b5, b6, b7 = (p.ps(f"A_pb{i}", [128, 512]) for i in range(8))
    pbk = [b0, b1, b2, b3, b4, b5, b6, b7]

    kd = "ExternalOutput" if debug else "Internal"
    aD = p.dram("A_aD", [D, TCORE], F32, kd)
    igD = p.dram("A_igD", [D, TCORE], F32, kd)
    vD = p.dram("A_vD", [D, TCORE], F32, kd)
    cdD = p.dram("A_cdD", [D, TCORE], F32, kd)
    inD = p.dram("A_inD", [D, TCORE], F32, kd)
    recD = p.dram("A_recD", [D, TCORE], BF16, kd)
    xaD = p.dram("A_xaD", [D, TCORE], F32, kd)
    flocD = p.dram("A_flocD", [1, 2048])
    hD = p.dram("A_hD", [D, TCORE], BF16, kd) if debug else None
    wbfD = p.dram("A_wbfD", [D, D], BF16, kd) if debug else None
    flocAG = p.dram("A_flocAG", [2, 2048])

    wguB = p.dram("A_wguB", [D, 16384], BF16)
    wdnB = p.dram("A_wdnB", [8192, D], BF16)
    with ExitStack() as ph:
        wtmp = [p.sb(f"A0_wt{k}", [128, 2048], F32, ph) for k in range(4)]
        wbtmp = [p.sb(f"A0_wb{k}", [128, 2048], BF16, ph) for k in range(4)]
        conv_bf(p, io["wgu"], wguB, D, 16384, wtmp, wbtmp)
        conv_bf(p, io["wdn"], wdnB, 8192, D, wtmp, wbtmp)
        ctx.phase()

    prew = p.sb("A_prew", [128, 16])
    dcb = p.sb("A_dcb", [128, 16])
    ffnw = p.sb("A_ffnw", [128, 16])
    wsel = p.sb("A_wsel", [128, 1])
    for t, key in [(prew, "prew"), (dcb, "dcb"), (ffnw, "ffnw"), (wsel, "wsel")]:
        p.load(t.h[:], io[key].ap[:], t.b, io[key].b)

    with ExitStack() as ph:
        h_sb = [p.sb(f"A_h{k}", [128, TCORE], BF16, ph) for k in range(16)]
        xin = [p.sb(f"A_x{k}", [128, 512], F32, ph) for k in range(16)]
        sqb = [p.sb(f"A_sq{k}", [128, 512], BF16, ph) for k in range(2)]
        rowt = p.sb("A_row", [1, 512], F32, ph)
        for tt in range(4):
            cs = slice(tt * 512, (tt + 1) * 512)
            for kc in range(16):
                p.load(xin[kc].h[:], io["x"].ap[kc * 128:(kc + 1) * 128, cs], xin[kc].b, io["x"].b)

            def put(kc, ap, tsrc, bcast, tt=tt, cs=cs):
                p.dve(lambda e, kc=kc: e.scalar_tensor_tensor(
                    h_sb[kc].h[:, cs], ap, prew.h[:, kc:kc + 1], bcast.h[:, :512],
                    ALU.mult, ALU.mult), h_sb[kc].b, [tsrc.b, prew.b, bcast.b])
            rmsnorm_fm(p, c, [(xin[k], xin[k].h[:]) for k in range(16)], prew, 0,
                       put, (b2, b3), 512, sqb, rowt)
        ctx.phase()
        if debug:
            for kc in range(16):
                p.store(hD.ap[kc * 128:(kc + 1) * 128, :], h_sb[kc].h[:], h_sb[kc].b, hD.b)

        wfp = [p.sb(f"A_wfp{k}", [128, 2048], F32, ph) for k in range(2)]
        wbf = [p.sb(f"A_wbf{k}", [128, 2048], BF16, ph) for k in range(16)]
        ev = [p.sb(f"A_ev{k}", [128, 512], F32, ph) for k in range(2)]
        for wkey, dst, func, bias in [("wa", aD, AF.Sigmoid, dcb), ("wi", igD, AF.Sigmoid, None),
                                      ("wv", vD, None, None)]:
            for kc in range(16):
                f = wfp[kc % 2]
                p.load(f.h[:], io[wkey].ap[kc * 128:(kc + 1) * 128, :], f.b, io[wkey].b)
                p.rop("dve", "tensor_copy", wbf[kc].h[:], f.h[:], outs=[wbf[kc].b], reads=[f.b])
            if debug and wkey == "wa":
                for kc in range(16):
                    p.store(wbfD.ap[kc * 128:(kc + 1) * 128, :], wbf[kc].h[:], wbf[kc].b, wbfD.b)
            for tt in range(4):
                cs = slice(tt * 512, (tt + 1) * 512)
                for oc in range(16):
                    bank = (b0, b1)[oc % 2]
                    for kc in range(16):
                        p.mm(bank.h[:], wbf[kc].h[:, oc * 128:(oc + 1) * 128], h_sb[kc].h[:, cs],
                             bank.b, [wbf[kc].b, h_sb[kc].b], start=(kc == 0), stop=(kc == 15))
                    o = ev[oc % 2]
                    if func is not None:
                        p.act(o.h[:], bank.h[:], func, o.b, [bank.b] + ([bias.b] if bias else []),
                              bias=(bias.h[:, oc:oc + 1] if bias else None))
                    else:
                        p.ctx.op("act", lambda e, o=o, bank=bank: e.copy(out=o.h[:], in_=bank.h[:]),
                                 reads=[bank.b], writes=[o.b])
                    p.store(dst.ap[oc * 128:(oc + 1) * 128, cs], o.h[:], o.b, dst.b)
        ctx.phase()

    ph2 = ExitStack()
    cdl = [p.sb(f"A_cdl{dc}", [128, 32], F32, ph2) for dc in range(16)]
    il = [p.sb(f"A_il{dc}", [128, 32], F32, ph2) for dc in range(16)]
    S0 = [p.sb(f"A_S0{dc}", [128, 32], F32, ph2) for dc in range(16)]
    Pp = [p.sb(f"A_P{dc}", [128, 32], F32, ph2) for dc in range(16)]
    floc = p.sb("A_floc", [128, 16], F32, ph2)
    sinit = p.sb("A_sinit", [128, 16], F32, ph2)
    with ExitStack() as ph:
        at = p.sb("A_at", [128, 2048], F32, ph)
        igt = p.sb("A_igt", [128, 2048], F32, ph)
        vt = p.sb("A_vt", [128, 2048], F32, ph)
        w1 = p.sb("A_w1", [128, 2048], F32, ph)
        w2 = p.sb("A_w2", [128, 2048], F32, ph)
        w3 = p.sb("A_w3", [128, 2048], F32, ph)
        for dc in range(16):
            rs = slice(dc * 128, (dc + 1) * 128)
            p.load(at.h[:], aD.ap[rs, :], at.b, aD.b)
            p.load(igt.h[:], igD.ap[rs, :], igt.b, igD.b)
            p.load(vt.h[:], vD.ap[rs, :], vt.b, vD.b)
            p.act(w1.h[:], at.h[:], AF.Square, w1.b, [at.b])
            p.dve(lambda e: e.tensor_scalar(w1.h[:], w1.h[:], -1.0, 1.0, ALU.mult, ALU.add), w1.b, [w1.b])
            p.dve(lambda e: e.tensor_scalar_max(w1.h[:], w1.h[:], 1e-8), w1.b, [w1.b])
            p.act(w1.h[:], w1.h[:], AF.Sqrt, w1.b, [w1.b])
            p.dve(lambda e: e.tensor_mul(w2.h[:], igt.h[:], vt.h[:]), w2.b, [igt.b, vt.b])
            p.dve(lambda e: e.tensor_mul(w2.h[:], w2.h[:], w1.h[:]), w2.b, [w2.b, w1.b])
            p.dve(lambda e: e.tensor_scalar_max(w1.h[:], at.h[:], 1e-10), w1.b, [at.b])
            p.act(w1.h[:], w1.h[:], AF.Ln, w1.b, [w1.b])
            for cc_ in range(32):
                s = slice(cc_ * 64, (cc_ + 1) * 64)
                p.dve(lambda e, s=s: e.tensor_tensor_scan(w3.h[:, s], c["ones64"].h[:], w1.h[:, s],
                                                          0.0, ALU.mult, ALU.add), w3.b,
                      [w1.b, c["ones64"].b])
            p.act(w3.h[:], w3.h[:], AF.Exp, w3.b, [w3.b])
            p.store(cdD.ap[rs, :], w3.h[:], w3.b, cdD.b)
            p.dve(lambda e: e.tensor_scalar_max(w1.h[:], w3.h[:], 1e-10), w1.b, [w3.b])
            p.dve(lambda e: e.reciprocal(w1.h[:], w1.h[:]), w1.b, [w1.b])
            p.dve(lambda e: e.tensor_mul(w2.h[:], w2.h[:], w1.h[:]), w2.b, [w2.b, w1.b])
            for cc_ in range(32):
                s = slice(cc_ * 64, (cc_ + 1) * 64)
                p.dve(lambda e, s=s: e.tensor_tensor_scan(w1.h[:, s], c["ones64"].h[:], w2.h[:, s],
                                                          0.0, ALU.mult, ALU.add), w1.b,
                      [w2.b, c["ones64"].b])
            p.dve(lambda e: e.tensor_mul(w1.h[:], w1.h[:], w3.h[:]), w1.b, [w1.b, w3.b])
            p.store(inD.ap[rs, :], w1.h[:], w1.b, inD.b)
            p.dve(lambda e, dc=dc: e.tensor_copy(cdl[dc].h[:], w3.h[:, 63:2048:64]), cdl[dc].b, [w3.b])
            p.dve(lambda e, dc=dc: e.tensor_copy(il[dc].h[:], w1.h[:, 63:2048:64]), il[dc].b, [w1.b])
            p.dve(lambda e, dc=dc: e.tensor_tensor_scan(S0[dc].h[:], cdl[dc].h[:], il[dc].h[:],
                                                        0.0, ALU.mult, ALU.add), S0[dc].b,
                  [cdl[dc].b, il[dc].b])
            p.dve(lambda e, dc=dc: e.tensor_tensor_scan(Pp[dc].h[:], cdl[dc].h[:], c["zero32"].h[:],
                                                        1.0, ALU.mult, ALU.add), Pp[dc].b,
                  [cdl[dc].b, c["zero32"].b])
            p.dve(lambda e, dc=dc: e.tensor_copy(floc.h[:, dc:dc + 1], S0[dc].h[:, 31:32]),
                  floc.b, [S0[dc].b])
        ctx.phase()
    p.store(flocD.ap[:], floc.h[:], floc.b, flocD.b)
    p.ctx.op("pool", lambda e: e.collective_compute(
        "AllGather", ALU.bypass, replica_groups=[[0, 1], [2, 3], [4, 5], [6, 7]],
        ins=[flocD.ap.opt()], outs=[flocAG.ap.opt()]),
        reads=[flocD.b], writes=[flocAG.b], cc=True)
    p.load(sinit.h[:], flocAG.ap[0:1, :].rearrange("o (p c) -> (o p) c", p=128), sinit.b, flocAG.b)
    p.dve(lambda e: e.tensor_scalar(sinit.h[:], sinit.h[:], wsel.h[:], None, ALU.mult),
          sinit.b, [sinit.b, wsel.b])

    with ExitStack() as ph:
        cdt = p.sb("A_cdt", [128, 2048], F32, ph)
        int_ = p.sb("A_int", [128, 2048], F32, ph)
        Sin = p.sb("A_Sin", [128, 32], F32, ph)
        Pin = p.sb("A_Pin", [128, 32], F32, ph)
        tmp = p.sb("A_tmp2", [128, 32], F32, ph)
        recb = p.sb("A_recb", [128, 2048], BF16, ph)
        ctmp = p.sb("A_ctmp", [128, 2048], F32, ph)
        for dc in range(16):
            rs = slice(dc * 128, (dc + 1) * 128)
            p.load(cdt.h[:], cdD.ap[rs, :], cdt.b, cdD.b)
            p.load(int_.h[:], inD.ap[rs, :], int_.b, inD.b)
            p.pool(lambda e: e.memset(Pin.h[:, 0:1], 1.0), Pin.b)
            p.dve(lambda e, dc=dc: e.tensor_copy(Pin.h[:, 1:32], Pp[dc].h[:, 0:31]), Pin.b,
                  [Pp[dc].b, Pin.b])
            p.pool(lambda e: e.memset(Sin.h[:, 0:1], 0.0), Sin.b)
            p.dve(lambda e, dc=dc: e.tensor_copy(Sin.h[:, 1:32], S0[dc].h[:, 0:31]), Sin.b,
                  [S0[dc].b, Sin.b])
            p.dve(lambda e, dc=dc: e.tensor_scalar(tmp.h[:], Pin.h[:], sinit.h[:, dc:dc + 1],
                                                   None, ALU.mult), tmp.b, [Pin.b, sinit.b])
            p.dve(lambda e: e.tensor_add(tmp.h[:], tmp.h[:], Sin.h[:]), tmp.b, [tmp.b, Sin.b])
            bc = bass.AP(tmp.h, 0, [[32, 128], [1, 32], [0, 64]])
            p.dve(lambda e, bc=bc: e.tensor_tensor(
                ctmp.h.ap().rearrange("p (c j) -> p c j", c=32), cdt.h.ap().rearrange(
                    "p (c j) -> p c j", c=32), bc, ALU.mult), ctmp.b, [cdt.b, tmp.b])
            p.dve(lambda e: e.tensor_add(recb.h[:], int_.h[:], ctmp.h[:]), recb.b, [int_.b, ctmp.b])
            p.store(recD.ap[rs, :], recb.h[:], recb.b, recD.b)
        ctx.phase()
    ph2.close()

    with ExitStack() as ph:
        wfp = [p.sb(f"A3_wfp{k}", [128, 2048], F32, ph) for k in range(2)]
        wbf = [p.sb(f"A3_wbf{k}", [128, 2048], BF16, ph) for k in range(16)]
        rsl = [p.sb(f"A3_r{k}", [128, 512], BF16, ph) for k in range(16)]
        xsl = [p.sb(f"A3_x{k}", [128, 512], F32, ph) for k in range(2)]
        ev = [p.sb(f"A3_ev{k}", [128, 512], F32, ph) for k in range(2)]
        for kc in range(16):
            f = wfp[kc % 2]
            p.load(f.h[:], io["wo"].ap[kc * 128:(kc + 1) * 128, :], f.b, io["wo"].b)
            p.rop("dve", "tensor_copy", wbf[kc].h[:], f.h[:], outs=[wbf[kc].b], reads=[f.b])
        for tt in range(4):
            cs = slice(tt * 512, (tt + 1) * 512)
            for kc in range(16):
                p.load(rsl[kc].h[:], recD.ap[kc * 128:(kc + 1) * 128, cs], rsl[kc].b, recD.b)
            for oc in range(16):
                bank = (b0, b1)[oc % 2]
                for kc in range(16):
                    p.mm(bank.h[:], wbf[kc].h[:, oc * 128:(oc + 1) * 128], rsl[kc].h[:],
                         bank.b, [wbf[kc].b, rsl[kc].b], start=(kc == 0), stop=(kc == 15))
                x0 = xsl[oc % 2]
                p.load(x0.h[:], io["x"].ap[oc * 128:(oc + 1) * 128, cs], x0.b, io["x"].b)
                o = ev[oc % 2]
                p.dve(lambda e, o=o, x0=x0, bank=bank: e.tensor_add(o.h[:], x0.h[:], bank.h[:]),
                      o.b, [x0.b, bank.b])
                p.store(xaD.ap[oc * 128:(oc + 1) * 128, cs], o.h[:], o.b, xaD.b)
        ctx.phase()

    with ExitStack() as ph:
        xa = [p.sb(f"A4_xa{k}", [128, 512], F32, ph) for k in range(16)]
        h2 = [p.sb(f"A4_h2{k}", [128, 512], BF16, ph) for k in range(16)]
        sqb = [p.sb(f"A4_sq{k}", [128, 512], BF16, ph) for k in range(2)]
        rowt = p.sb("A4_row", [1, 512], F32, ph)
        rowt2 = p.sb("A4_row2", [1, 512], F32, ph)
        wpool = [p.sb(f"A4_wp{k}", [128, 1024], BF16, ph) for k in range(16)]
        gsb = [p.sb(f"A4_gs{k}", [128, 512], F32, ph) for k in range(8)]
        inter = [p.sb(f"A4_in{k}", [128, 512], BF16, ph) for k in range(8)]
        ybf = [p.sb(f"A4_ybf{k}", [128, 512], BF16, ph) for k in range(2)]
        ev = [p.sb(f"A4_ev{k}", [128, 512], F32, ph) for k in range(2)]
        wi_ = [0]

        def wnext():
            wi_[0] += 1
            return wpool[wi_[0] % 16]
        for tt in range(4):
            cs = slice(tt * 512, (tt + 1) * 512)
            for kc in range(16):
                p.load(xa[kc].h[:], xaD.ap[kc * 128:(kc + 1) * 128, cs], xa[kc].b, xaD.b)

            def put(kc, ap, tsrc, bcast):
                p.dve(lambda e, kc=kc: e.scalar_tensor_tensor(
                    h2[kc].h[:], ap, ffnw.h[:, kc:kc + 1], bcast.h[:, :512],
                    ALU.mult, ALU.mult), h2[kc].b, [tsrc.b, ffnw.b, bcast.b])
            rmsnorm_fm(p, c, [(xa[k], xa[k].h[:]) for k in range(16)], ffnw, 0,
                       put, (b2, b3), 512, sqb, rowt)
            for g in range(8):
                for kc in range(16):
                    f = wnext()
                    p.load(f.h[:], wguB.ap[kc * 128:(kc + 1) * 128,
                                           g * 1024:(g + 1) * 1024], f.b, wguB.b)
                    for j in range(8):
                        p.mm(pbk[j].h[:], f.h[:, j * 128:(j + 1) * 128], h2[kc].h[:],
                             pbk[j].b, [f.b, h2[kc].b], start=(kc == 0), stop=(kc == 15))
                for j in range(8):
                    p.act(gsb[j].h[:], pbk[j].h[:], AF.Silu, gsb[j].b, [pbk[j].b])
                for kc in range(16):
                    f = wnext()
                    p.load(f.h[:], wguB.ap[kc * 128:(kc + 1) * 128,
                                           8192 + g * 1024:8192 + (g + 1) * 1024], f.b, wguB.b)
                    for j in range(8):
                        p.mm(pbk[j].h[:], f.h[:, j * 128:(j + 1) * 128], h2[kc].h[:],
                             pbk[j].b, [f.b, h2[kc].b], start=(kc == 0), stop=(kc == 15))
                for j in range(8):
                    p.rop("dve", "tensor_mul", inter[j].h[:], gsb[j].h[:], pbk[j].h[:],
                          outs=[inter[j].b], reads=[gsb[j].b, pbk[j].b])
                for half in range(2):
                    for j in range(8):
                        f = wnext()
                        p.load(f.h[:], wdnB.ap[g * 1024 + j * 128:g * 1024 + (j + 1) * 128,
                                               half * 1024:(half + 1) * 1024], f.b, wdnB.b)
                        for oc in range(8):
                            p.mm(pbk[oc].h[:], f.h[:, oc * 128:(oc + 1) * 128], inter[j].h[:],
                                 pbk[oc].b, [f.b, inter[j].b], start=(j == 0), stop=(j == 7))
                    for oc in range(8):
                        o = half * 8 + oc
                        p.rop("dve", "tensor_add", xa[o].h[:], xa[o].h[:], pbk[oc].h[:],
                              outs=[xa[o].b], reads=[xa[o].b, pbk[oc].b])
            for oc in range(16):
                xv = ev[oc % 2]
                p.load(xv.h[:], io["x"].ap[oc * 128:(oc + 1) * 128, cs], xv.b, io["x"].b)
                p.dve(lambda e, oc=oc, xv=xv: e.tensor_add(xa[oc].h[:], xa[oc].h[:], xv.h[:]),
                      xa[oc].b, [xa[oc].b, xv.b])

            def putA(oc, bm, br, cs=cs):
                o = ev[oc % 2]
                p.rop("dve", "tensor_sub", o.h[:], xa[oc].h[:], bm.h[:], outs=[o.b],
                      reads=[xa[oc].b, bm.b])
                p.rop("dve", "tensor_mul", o.h[:], o.h[:], br.h[:], outs=[o.b], reads=[o.b, br.b])
                p.store(io["x1"].ap[oc * 128:(oc + 1) * 128, cs], o.h[:], o.b, io["x1"].b)
            ln_fm(p, c, xa, (b2, b3, b4, b5), [sqb[0], sqb[1], ybf[0], ybf[1]],
                  rowt, rowt2, putA)
        ctx.phase()


def build_launch_A(debug=False):
    nc = bass.Bass("TRN2", target_bir_lowering=False, debug=False)
    p = P(nc)
    io = {}
    io["x"] = DT(nc.dram_tensor("x", [D, TCORE], F32, kind="ExternalInput").ap(), "x")
    for nm, shp in [("wa", [D, D]), ("wi", [D, D]), ("wv", [D, D]), ("wo", [D, D]),
                    ("wgu", [D, 16384]), ("wdn", [8192, D])]:
        io[nm] = DT(nc.dram_tensor(nm, shp, F32, kind="ExternalInput").ap(), nm)
    for nm in ["prew", "dcb", "ffnw"]:
        io[nm] = DT(nc.dram_tensor(nm, [128, 16], F32, kind="ExternalInput").ap(), nm)
    io["wsel"] = DT(nc.dram_tensor("wsel", [128, 1], F32, kind="ExternalInput").ap(), "wsel")
    io["x1"] = DT(nc.dram_tensor("x1", [D, TCORE], F32, kind="ExternalOutput").ap(), "x1")
    with p.es:
        c = _mk_consts(p)
        emit_l0_attn(p, c, io, debug=debug)
        p.ctx.finish()
        p.ctx.build(p.es)
    return nc




def _slices_fm(x):
    if isinstance(x, list):
        return x
    out = []
    for cid in range(NCORE):
        b, h = cid // 2, cid % 2
        out.append(np.ascontiguousarray(np.asarray(x)[b, h * 2048:(h + 1) * 2048, :].T))
    return out


def _fm(w):
    return np.ascontiguousarray(w.reshape(-1, 128).T.astype(np.float32))


def prep_inputs_A(ins):
    x = np.asarray(ins["L0_attn_x"], np.float32)
    maps = []
    for cid in range(NCORE):
        b, h = cid // 2, cid % 2
        xs = np.ascontiguousarray(x[b, h * 2048:(h + 1) * 2048, :].T)
        maps.append({
            "x": xs,
            "wa": np.asarray(ins["L0_attn_w_a"], np.float32),
            "wi": np.asarray(ins["L0_attn_w_i"], np.float32),
            "wv": np.asarray(ins["L0_attn_w_v"], np.float32),
            "wo": np.asarray(ins["L0_attn_out_proj_w"], np.float32),
            "wgu": np.asarray(ins["L0_attn_w_gate_up"], np.float32),
            "wdn": np.asarray(ins["L0_attn_w_down"], np.float32),
            "prew": _fm(np.asarray(ins["L0_attn_pre_norm_w"], np.float32)),
            "dcb": _fm(np.asarray(ins["L0_attn_decay_bias"], np.float32)),
            "ffnw": _fm(np.asarray(ins["L0_attn_ffn_norm_w"], np.float32)),
            "wsel": np.full((128, 1), float(h), np.float32),
        })
    return maps


_NC_CACHE = {}


def _nc_for(name, builder):
    if name not in _NC_CACHE:
        _NC_CACHE[name] = builder()
    return _NC_CACHE[name]


def kernel_multi(**inputs):
    cores = list(range(NCORE))
    resA = run_bass_kernel_spmd(_nc_for("A", build_launch_A), prep_inputs_A(inputs),
                                core_ids=cores)
    x1c = [np.asarray(resA.results[c]["x1"]) for c in cores]
    resB = run_bass_kernel_spmd(_nc_for("B", build_launch_B), prep_inputs_B(inputs, x1c),
                                core_ids=cores)
    x2c = [np.asarray(resB.results[c]["x2"]) for c in cores]
    resC = run_bass_kernel_spmd(_nc_for("C", build_launch_C), prep_inputs_C(inputs, x2c),
                                core_ids=cores)
    x3c = [np.asarray(resC.results[c]["x3"]) for c in cores]
    resD = run_bass_kernel_spmd(_nc_for("D", build_launch_D), prep_inputs_D(inputs, x3c),
                                core_ids=cores)
    out = np.empty((4, 4096, 2048), np.float32)
    for cid in cores:
        b, h = cid // 2, cid % 2
        out[b, h * 2048:(h + 1) * 2048, :] = resD.results[cid]["out"].T
    return out


HID = [1024, 2048, 3072, 4096, 1024, 2048, 3072, 4096]


def ln_fm(p, c, y, banks, sqb, rowt, rowt2, put, t_w=512):
    s1, s2, bm, br = banks
    for kc in range(16):
        yb = sqb[0] if kc % 2 == 0 else sqb[1]
        p.rop("dve", "tensor_copy", yb.h[:, :t_w], y[kc].h[:, :t_w], outs=[yb.b], reads=[y[kc].b])
        p.mm(s1.h[0:1, :t_w], c["ones_col_bf"].h[:], yb.h[:, :t_w], s1.b,
             [yb.b, c["ones_col_bf"].b], start=(kc == 0), stop=(kc == 15))
        sq = sqb[2] if kc % 2 == 0 else sqb[3]
        p.act(sq.h[:, :t_w], y[kc].h[:, :t_w], AF.Square, sq.b, [y[kc].b])
        p.mm(s2.h[0:1, :t_w], c["ones_col_bf"].h[:], sq.h[:, :t_w], s2.b,
             [sq.b, c["ones_col_bf"].b], start=(kc == 0), stop=(kc == 15))
    p.rop("dve", "tensor_scalar_mul", rowt.h[0:1, :t_w], s1.h[0:1, :t_w], 1.0 / 2048.0,
          outs=[rowt.b], reads=[s1.b])
    p.rop("dve", "scalar_tensor_tensor", rowt2.h[0:1, :t_w], rowt.h[0:1, :t_w], 1.0,
          rowt.h[0:1, :t_w], ALU.mult, ALU.mult, outs=[rowt2.b], reads=[rowt.b])
    p.rop("dve", "scalar_tensor_tensor", rowt2.h[0:1, :t_w], s2.h[0:1, :t_w], 1.0 / 2048.0,
          rowt2.h[0:1, :t_w], ALU.mult, ALU.subtract, outs=[rowt2.b], reads=[s2.b, rowt2.b])
    p.act(rowt2.h[0:1, :t_w], rowt2.h[0:1, :t_w], AF.Sqrt, rowt2.b, [rowt2.b, c["eps5"].b],
          bias=c["eps5"].h[0:1, :])
    p.rop("dve", "reciprocal", rowt2.h[0:1, :t_w], rowt2.h[0:1, :t_w],
          outs=[rowt2.b], reads=[rowt2.b])
    p.mm(bm.h[:, :t_w], c["ones_row_f"].h[:], rowt.h[0:1, :t_w], bm.b,
         [rowt.b, c["ones_row_f"].b], start=True, stop=True)
    p.mm(br.h[:, :t_w], c["ones_row_f"].h[:], rowt2.h[0:1, :t_w], br.b,
         [rowt2.b, c["ones_row_f"].b], start=True, stop=True)
    for oc in range(16):
        put(oc, bm, br)


def conv_bf(p, src, dst, R, C, wtmp, wbtmp):
    i = 0
    for r0 in range(0, R, 128):
        for c0 in range(0, C, 2048):
            cw_ = min(2048, C - c0)
            f = wtmp[i % len(wtmp)]
            g = wbtmp[i % len(wbtmp)]
            i += 1
            p.load(f.h[:, :cw_], src.ap[r0:r0 + 128, c0:c0 + cw_], f.b, src.b)
            p.rop("dve", "tensor_copy", g.h[:, :cw_], f.h[:, :cw_], outs=[g.b], reads=[f.b])
            p.store(dst.ap[r0:r0 + 128, c0:c0 + cw_], g.h[:, :cw_], g.b, dst.b)


def emit_l0_ffn(p, c, io, debug=False):
    ctx = p.ctx
    pb = [p.ps(f"B_pb{i}", [128, 512]) for i in range(8)]

    wB = {}
    shapes = {"dw": (2048, 1024), "uw": (1024, 2048), "s1": (1024, 1024),
              "s2": (1024, 1024), "cw": (2048, 2048)}
    for i, hh in enumerate(HID):
        shapes[f"e{i}w1"] = (1024, hh)
        shapes[f"e{i}w2"] = (hh, 1024)
    for k, (R, C) in shapes.items():
        wB[k] = p.dram(f"B_{k}B", [R, C], BF16)

    rwsb = [p.sb(f"B_rw{kc}", [128, 8], BF16) for kc in range(16)]
    dbf = p.sb("B_dbf", [128, 8])
    sb1f = p.sb("B_sb1f", [128, 8])
    b1f = [p.sb(f"B_b1f{i}", [128, HID[i] // 128]) for i in range(8)]
    sb2s = p.sb("B_sb2s", [1, 1024])
    ubr = p.sb("B_ubr", [1, 2048])
    cbr = p.sb("B_cbr", [1, 2048])
    rbr = p.sb("B_rbr", [1, 8])
    negc = p.sb("B_negc", [1, 8])
    tbeps = p.sb("B_tbeps", [128, 8])
    idf = p.sb("B_idf", [128, 128])
    p.load(dbf.h[:], io["dbf"].ap[:], dbf.b, io["dbf"].b)
    p.load(sb1f.h[:], io["sb1f"].ap[:], sb1f.b, io["sb1f"].b)
    for i in range(8):
        p.load(b1f[i].h[:], io[f"e{i}b1f"].ap[:], b1f[i].b, io[f"e{i}b1f"].b)
    p.load(sb2s.h[:], io["sb2r"].ap[:], sb2s.b, io["sb2r"].b)
    p.rop("dve", "tensor_scalar_mul", sb2s.h[:], sb2s.h[:], 0.1, outs=[sb2s.b], reads=[sb2s.b])
    for t, key in [(ubr, "ubr"), (cbr, "cbr"), (rbr, "rbr"), (negc, "negc"),
                   (tbeps, "tbeps"), (idf, "idf")]:
        p.load(t.h[:], io[key].ap[:], t.b, io[key].b)
    c["ones_row512"] = p.sb("c_ones_row512", [1, 512])
    t_ = c["ones_row512"]
    p.rop("pool", "memset", t_.h[:], 1.0, outs=[t_.b])

    with ExitStack() as ph:
        wtmp = [p.sb(f"B0_wt{k}", [128, 2048], F32, ph) for k in range(2)]
        wbtmp = [p.sb(f"B0_wb{k}", [128, 2048], BF16, ph) for k in range(2)]
        for k, (R, C) in shapes.items():
            conv_bf(p, io[k], wB[k], R, C, wtmp, wbtmp)
        for kc in range(16):
            f = wtmp[kc % 2]
            p.load(f.h[:, :8], io["rw"].ap[kc * 128:(kc + 1) * 128, :], f.b, io["rw"].b)
            p.rop("dve", "tensor_copy", rwsb[kc].h[:], f.h[:, :8], outs=[rwsb[kc].b], reads=[f.b])
        ctx.phase()

    with ExitStack() as ph:
        x = [p.sb(f"B_x{k}", [128, 512], F32, ph) for k in range(16)]
        poolA = [p.sb(f"B_pA{k}", [128, 512], BF16, ph) for k in range(32)]
        poolB = [p.sb(f"B_pB{k}", [128, 512], BF16, ph) for k in range(24)]
        xbf = poolA[0:16]
        g1s = poolA
        G = poolB[0:8]
        ub = poolB[8:16]
        Gx = poolB[0:16]
        ombf = poolB[16:24]
        macc = [p.sb(f"B_macc{k}", [128, 512], F32, ph) for k in range(8)]
        gb = [p.sb(f"B_gb{k}", [128, 512], F32, ph) for k in range(2)]
        mrow = [[p.sb(f"B_mr{kk}_{e}", [1, 512], BF16, ph) for e in range(8)] for kk in range(2)]
        grow = [p.sb(f"B_gr{kk}", [1, 512], F32, ph) for kk in range(2)]
        w1rot = [p.sb(f"B_w1r{k}", [128, 1024], BF16, ph) for k in range(4)]
        w2rot = [p.sb(f"B_w2r{k}", [128, 1024], BF16, ph) for k in range(4)]
        wrot = [p.sb(f"B_wr{k}", [128, 1024], BF16, ph) for k in range(4)]
        b2rot = [p.sb(f"B_b2r{k}_", [1, 1024], F32, ph) for k in range(2)]
        gact = [p.sb(f"B_gact{k}", [128, 512], F32, ph) for k in range(2)]
        sqb = [p.sb(f"B_sq{k}", [128, 512], BF16, ph) for k in range(4)]
        rowt = p.sb("B_row", [1, 512], F32, ph)
        rowt2 = p.sb("B_row2", [1, 512], F32, ph)
        ev = [p.sb(f"B_ev{k}", [128, 512], F32, ph) for k in range(2)]
        r8 = [p.sb(f"B_r8{k}", [128, 8], F32, ph) for k in range(4)]
        r1 = [p.sb(f"B_r1{k}", [128, 1], F32, ph) for k in range(6)]

        for tt in range(4):
            cs = slice(tt * 512, (tt + 1) * 512)
            for kc in range(16):
                p.load(x[kc].h[:], io["x1"].ap[kc * 128:(kc + 1) * 128, cs], x[kc].b, io["x1"].b)
                p.rop("dve", "tensor_copy", xbf[kc].h[:], x[kc].h[:], outs=[xbf[kc].b],
                      reads=[x[kc].b])
            i = 0
            for kc in range(16):
                f = wrot[kc % 4]
                p.load(f.h[:, :1024], wB["dw"].ap[kc * 128:(kc + 1) * 128, :], f.b, wB["dw"].b)
                for oc in range(8):
                    p.mm(pb[oc].h[:], f.h[:, oc * 128:(oc + 1) * 128], xbf[kc].h[:],
                         pb[oc].b, [f.b, xbf[kc].b], start=(kc == 0), stop=(kc == 15))
            for oc in range(8):
                p.act(G[oc].h[:], pb[oc].h[:], AF.Gelu, G[oc].b, [pb[oc].b, dbf.b],
                      bias=dbf.h[:, oc:oc + 1])
            for ts in range(4):
                tsl = slice(ts * 128, (ts + 1) * 128)
                pr = pb[0]
                for kc in range(16):
                    p.mm(pr.h[:, :8], xbf[kc].h[:, tsl], rwsb[kc].h[:], pr.b,
                         [xbf[kc].b, rwsb[kc].b], start=(kc == 0), stop=False)
                p.mm(pr.h[:, :8], c["ones_row_f"].h[:], rbr.h[:], pr.b,
                     [c["ones_row_f"].b, rbr.b], start=False, stop=False)
                p.mm(pr.h[:, :8], c["ones_row_f"].h[:], negc.h[:], pr.b,
                     [c["ones_row_f"].b, negc.b], start=False, stop=True)
                pn, tb, m1c, msk = r8[0], r8[1], r1[0], r8[2]
                p.rop("dve", "reduce_max", m1c.h[:], pr.h[:, :8], AX.X,
                      outs=[m1c.b], reads=[pr.b])
                p.rop("dve", "tensor_scalar_mul", m1c.h[:], m1c.h[:], -1.0,
                      outs=[m1c.b], reads=[m1c.b])
                p.act(pn.h[:], pr.h[:, :8], AF.Exp, pn.b, [pr.b, m1c.b],
                      bias=m1c.h[:], accum=r1[1].h[:], accumb=r1[1].b)
                p.rop("dve", "reciprocal", r1[1].h[:], r1[1].h[:], outs=[r1[1].b], reads=[r1[1].b])
                p.rop("dve", "tensor_scalar", pn.h[:], pn.h[:], r1[1].h[:], None, ALU.mult,
                      outs=[pn.b], reads=[pn.b, r1[1].b])
                p.rop("dve", "tensor_add", tb.h[:], pn.h[:], tbeps.h[:], outs=[tb.b],
                      reads=[pn.b, tbeps.b])
                for kk in range(2):
                    mx, tv = r1[2 + kk], r1[4 + kk]
                    p.rop("dve", "reduce_max", mx.h[:], tb.h[:], AX.X, outs=[mx.b], reads=[tb.b])
                    p.rop("dve", "tensor_scalar", msk.h[:], tb.h[:], mx.h[:], None, ALU.is_ge,
                          outs=[msk.b], reads=[tb.b, mx.b])
                    p.rop("dve", "tensor_mul", r8[3].h[:], msk.h[:], pn.h[:],
                          outs=[r8[3].b], reads=[msk.b, pn.b])
                    p.rop("dve", "reduce_sum", tv.h[:], r8[3].h[:], AX.X, outs=[tv.b],
                          reads=[r8[3].b])
                    if kk == 0:
                        p.rop("dve", "scalar_tensor_tensor", tb.h[:], msk.h[:], -1e9, tb.h[:],
                              ALU.mult, ALU.add, outs=[tb.b], reads=[msk.b, tb.b])
                    for e in range(8):
                        p.ctx.op("pe", (lambda ee, o=pb[6].h[0:1, :128], i_=msk.h[:, e:e + 1],
                                        idf_=idf.h[:]: ee.transpose(o, i_, idf_)),
                                 reads=[msk.b, idf.b], writes=[pb[6].b])
                        p.rop("dve", "tensor_copy", mrow[kk][e].h[0:1, tsl], pb[6].h[0:1, :128],
                              outs=[mrow[kk][e].b], reads=[pb[6].b])
                g1c, g2c = r1[2], r1[3]
                p.act(g1c.h[:], r1[4].h[:], AF.Exp, g1c.b, [r1[4].b])
                p.act(g2c.h[:], r1[5].h[:], AF.Exp, g2c.b, [r1[5].b])
                p.rop("dve", "tensor_add", r1[1].h[:], g1c.h[:], g2c.h[:], outs=[r1[1].b],
                      reads=[g1c.b, g2c.b])
                p.rop("dve", "reciprocal", r1[1].h[:], r1[1].h[:], outs=[r1[1].b], reads=[r1[1].b])
                for kk, gc in enumerate([g1c, g2c]):
                    p.rop("dve", "tensor_mul", gc.h[:], gc.h[:], r1[1].h[:], outs=[gc.b],
                          reads=[gc.b, r1[1].b])
                    p.ctx.op("pe", (lambda ee, o=pb[6].h[0:1, :128], i_=gc.h[:],
                                    idf_=idf.h[:]: ee.transpose(o, i_, idf_)),
                             reads=[gc.b, idf.b], writes=[pb[6].b])
                    p.rop("dve", "tensor_copy", grow[kk].h[0:1, tsl], pb[6].h[0:1, :128],
                          outs=[grow[kk].b], reads=[pb[6].b])
            for kk in range(2):
                p.mm(pb[7].h[:], c["ones_row_f"].h[:], grow[kk].h[:], pb[7].b,
                     [c["ones_row_f"].b, grow[kk].b], start=True, stop=True)
                p.rop("dve", "tensor_copy", gb[kk].h[:], pb[7].h[:], outs=[gb[kk].b],
                      reads=[pb[7].b])
            first_acc = [True]
            for kk in range(2):
                for e in range(8):
                    hh = HID[e]
                    noh = hh // 128
                    p.mm(pb[7].h[:], c["ones_row_bf"].h[:], mrow[kk][e].h[:], pb[7].b,
                         [c["ones_row_bf"].b, mrow[kk][e].b], start=True, stop=True)
                    for kc in range(8):
                        p.rop("dve", "tensor_mul", ub[kc].h[:], G[kc].h[:], pb[7].h[:],
                              outs=[ub[kc].b], reads=[G[kc].b, pb[7].b])
                    for g0 in range(0, noh, 8):
                        gn = min(8, noh - g0)
                        for kc in range(8):
                            f = w1rot[kc % 4]
                            p.load(f.h[:, :gn * 128],
                                   wB[f"e{e}w1"].ap[kc * 128:(kc + 1) * 128,
                                                    g0 * 128:(g0 + gn) * 128],
                                   f.b, wB[f"e{e}w1"].b)
                            for j in range(gn):
                                p.mm(pb[j].h[:], f.h[:, j * 128:(j + 1) * 128], ub[kc].h[:],
                                     pb[j].b, [f.b, ub[kc].b], start=(kc == 0), stop=(kc == 7))
                        for j in range(gn):
                            oh = g0 + j
                            ga = gact[j % 2]
                            p.act(ga.h[:], pb[j].h[:], AF.Gelu, ga.b, [pb[j].b, b1f[e].b],
                                  bias=b1f[e].h[:, oh:oh + 1])
                            p.rop("dve", "tensor_mul", g1s[oh].h[:], ga.h[:], gb[kk].h[:],
                                  outs=[g1s[oh].b], reads=[ga.b, gb[kk].b])
                    for oh in range(noh):
                        f = w2rot[oh % 4]
                        p.load(f.h[:], wB[f"e{e}w2"].ap[oh * 128:(oh + 1) * 128, :], f.b,
                               wB[f"e{e}w2"].b)
                        for oc in range(8):
                            p.mm(pb[oc].h[:], f.h[:, oc * 128:(oc + 1) * 128], g1s[oh].h[:],
                                 pb[oc].b, [f.b, g1s[oh].b], start=(oh == 0), stop=False)
                    b2t = b2rot[(kk * 8 + e) % 2]
                    p.load(b2t.h[:], io[f"e{e}b2r"].ap[:], b2t.b, io[f"e{e}b2r"].b)
                    for oc in range(8):
                        p.mm(pb[oc].h[:], b2t.h[0:1, oc * 128:(oc + 1) * 128], grow[kk].h[:],
                             pb[oc].b, [b2t.b, grow[kk].b], start=False, stop=True)
                    for oc in range(8):
                        if first_acc[0]:
                            p.rop("dve", "tensor_copy", macc[oc].h[:], pb[oc].h[:],
                                  outs=[macc[oc].b], reads=[pb[oc].b])
                        else:
                            p.rop("dve", "tensor_add", macc[oc].h[:], macc[oc].h[:], pb[oc].h[:],
                                  outs=[macc[oc].b], reads=[macc[oc].b, pb[oc].b])
                    first_acc[0] = False
            for kc in range(8):
                f = w1rot[kc % 4]
                p.load(f.h[:, :1024], wB["s1"].ap[kc * 128:(kc + 1) * 128, :],
                       f.b, wB["s1"].b)
                for j in range(8):
                    p.mm(pb[j].h[:], f.h[:, j * 128:(j + 1) * 128], G[kc].h[:],
                         pb[j].b, [f.b, G[kc].b], start=(kc == 0), stop=(kc == 7))
            for j in range(8):
                oh = j
                ga = gact[j % 2]
                p.act(ga.h[:], pb[j].h[:], AF.Gelu, ga.b, [pb[j].b, sb1f.b],
                      bias=sb1f.h[:, oh:oh + 1])
                p.rop("dve", "tensor_scalar_mul", g1s[oh].h[:], ga.h[:], 0.1,
                      outs=[g1s[oh].b], reads=[ga.b])
            for oh in range(8):
                f = w2rot[oh % 4]
                p.load(f.h[:], wB["s2"].ap[oh * 128:(oh + 1) * 128, :], f.b, wB["s2"].b)
                for oc in range(8):
                    p.mm(pb[oc].h[:], f.h[:, oc * 128:(oc + 1) * 128], g1s[oh].h[:],
                         pb[oc].b, [f.b, g1s[oh].b], start=(oh == 0), stop=False)
            for oc in range(8):
                p.mm(pb[oc].h[:], sb2s.h[0:1, oc * 128:(oc + 1) * 128], c["ones_row512"].h[:],
                     pb[oc].b, [sb2s.b, c["ones_row512"].b], start=False, stop=True)
            for oc in range(8):
                p.rop("dve", "tensor_add", macc[oc].h[:], macc[oc].h[:], pb[oc].h[:],
                      outs=[macc[oc].b], reads=[macc[oc].b, pb[oc].b])
            for kc in range(8):
                p.rop("dve", "tensor_copy", ombf[kc].h[:], macc[kc].h[:], outs=[ombf[kc].b],
                      reads=[macc[kc].b])
            for kc in range(16):
                p.act(Gx[kc].h[:], x[kc].h[:], AF.Gelu, Gx[kc].b, [x[kc].b])
            for half in range(2):
                oc0 = half * 8
                csl = slice(oc0 * 128, (oc0 + 8) * 128)
                for kc in range(8):
                    f = wrot[kc % 4]
                    p.load(f.h[:, :1024], wB["uw"].ap[kc * 128:(kc + 1) * 128, csl], f.b,
                           wB["uw"].b)
                    for j in range(8):
                        p.mm(pb[j].h[:], f.h[:, j * 128:(j + 1) * 128], ombf[kc].h[:],
                             pb[j].b, [f.b, ombf[kc].b], start=(kc == 0), stop=False)
                for kc in range(16):
                    f = wrot[kc % 4]
                    p.load(f.h[:, :1024], wB["cw"].ap[kc * 128:(kc + 1) * 128, csl], f.b,
                           wB["cw"].b)
                    for j in range(8):
                        p.mm(pb[j].h[:], f.h[:, j * 128:(j + 1) * 128], Gx[kc].h[:],
                             pb[j].b, [f.b, Gx[kc].b], start=False, stop=False)
                for j in range(8):
                    oc = oc0 + j
                    p.mm(pb[j].h[:], ubr.h[0:1, oc * 128:(oc + 1) * 128], c["ones_row512"].h[:],
                         pb[j].b, [ubr.b, c["ones_row512"].b], start=False, stop=False)
                    p.mm(pb[j].h[:], cbr.h[0:1, oc * 128:(oc + 1) * 128], c["ones_row512"].h[:],
                         pb[j].b, [cbr.b, c["ones_row512"].b], start=False, stop=True)
                for j in range(8):
                    oc = oc0 + j
                    p.rop("dve", "tensor_add", x[oc].h[:], x[oc].h[:], pb[j].h[:],
                          outs=[x[oc].b], reads=[x[oc].b, pb[j].b])

            def putB(oc, bm, br, cs=cs):
                o = ev[oc % 2]
                p.rop("dve", "tensor_sub", o.h[:], x[oc].h[:], bm.h[:], outs=[o.b],
                      reads=[x[oc].b, bm.b])
                p.rop("dve", "tensor_mul", o.h[:], o.h[:], br.h[:], outs=[o.b], reads=[o.b, br.b])
                p.store(io["x2"].ap[oc * 128:(oc + 1) * 128, cs], o.h[:], o.b, io["x2"].b)
            ln_fm(p, c, x, (pb[0], pb[1], pb[2], pb[3]), sqb, rowt, rowt2, putB)
            ctx.phase()


def build_launch_B(debug=False):
    nc = bass.Bass("TRN2", target_bir_lowering=False, debug=False)
    p = P(nc)
    io = {}

    def inp(nm, shp):
        io[nm] = DT(nc.dram_tensor(nm, list(shp), F32, kind="ExternalInput").ap(), nm)
    inp("x1", [D, TCORE])
    inp("dw", [2048, 1024]); inp("uw", [1024, 2048]); inp("rw", [2048, 8])
    inp("s1", [1024, 1024]); inp("s2", [1024, 1024]); inp("cw", [2048, 2048])
    for i, hh in enumerate(HID):
        inp(f"e{i}w1", [1024, hh]); inp(f"e{i}w2", [hh, 1024])
        inp(f"e{i}b1f", [128, hh // 128]); inp(f"e{i}b2r", [1, 1024])
    inp("dbf", [128, 8]); inp("sb1f", [128, 8]); inp("sb2r", [1, 1024])
    inp("ubr", [1, 2048]); inp("cbr", [1, 2048]); inp("rbr", [1, 8])
    inp("negc", [1, 8]); inp("tbeps", [128, 8]); inp("idf", [128, 128])
    io["x2"] = DT(nc.dram_tensor("x2", [D, TCORE], F32, kind="ExternalOutput").ap(), "x2")
    with p.es:
        c = _mk_consts(p)
        emit_l0_ffn(p, c, io, debug=debug)
        p.ctx.finish()
        p.ctx.build(p.es)
    return nc


def prep_inputs_B(ins, x1):
    f32 = np.float32
    com = {
        "dw": np.asarray(ins["L0_ffn_down_w"], f32),
        "uw": np.asarray(ins["L0_ffn_up_w"], f32),
        "rw": np.asarray(ins["L0_ffn_router_w"], f32),
        "s1": np.asarray(ins["L0_ffn_shared_w1"], f32),
        "s2": np.asarray(ins["L0_ffn_shared_w2"], f32),
        "cw": np.asarray(ins["L0_ffn_core_w"], f32),
        "dbf": _fm(np.asarray(ins["L0_ffn_down_b"], f32)),
        "sb1f": _fm(np.asarray(ins["L0_ffn_shared_b1"], f32)),
        "sb2r": np.asarray(ins["L0_ffn_shared_b2"], f32).reshape(1, 1024),
        "ubr": np.asarray(ins["L0_ffn_up_b"], f32).reshape(1, 2048),
        "cbr": np.asarray(ins["L0_ffn_core_b"], f32).reshape(1, 2048),
        "rbr": np.asarray(ins["L0_ffn_router_b"], f32).reshape(1, 8),
        "negc": (-1e-7 * np.array([2 * 1024 * hh for hh in HID], f32)).reshape(1, 8),
        "tbeps": np.tile(((7 - np.arange(8)) * 1e-7).astype(f32), (128, 1)),
        "idf": np.eye(128, dtype=f32),
    }
    for i in range(8):
        com[f"e{i}w1"] = np.asarray(ins[f"L0_ffn_e{i}_w1"], f32)
        com[f"e{i}w2"] = np.asarray(ins[f"L0_ffn_e{i}_w2"], f32)
        com[f"e{i}b1f"] = _fm(np.asarray(ins[f"L0_ffn_e{i}_b1"], f32))
        com[f"e{i}b2r"] = np.asarray(ins[f"L0_ffn_e{i}_b2"], f32).reshape(1, 1024)
    xs = _slices_fm(x1)
    maps = []
    for cid in range(NCORE):
        m = dict(com)
        m["x1"] = xs[cid]
        maps.append(m)
    return maps


ASCALE = float(128.0 ** -0.5)


def emit_l1_attn(p, c, io, debug=False):
    ctx = p.ctx
    pb = [p.ps(f"C_pb{i}", [128, 512]) for i in range(7)]
    pbf = p.ps("C_pbf", [128, 1024], BF16)

    x2bfD = p.dram("C_x2bf", [D, TCORE], BF16)
    xAG = p.dram("C_xAG", [2, D, TCORE], BF16)
    ctxD = p.dram("C_ctx", [1024, 4096], BF16)
    outPD = p.dram("C_outP", [D, 4096], BF16)
    outRD = p.dram("C_outR", [D, 4096], BF16)

    band = p.sb("C_band", [128, 896])
    idb = p.sb("C_idb", [128, 128], BF16)
    sel0 = p.sb("C_sel0", [128, 1])
    sel1 = p.sb("C_sel1", [128, 1])
    for t, key in [(band, "Band"), (idb, "IDB"), (sel0, "sel0"), (sel1, "sel1")]:
        p.load(t.h[:], io[key].ap[:], t.b, io[key].b)

    with ExitStack() as ph:
        xt = [p.sb(f"C1_x{k}", [128, 512], F32, ph) for k in range(2)]
        xb = [p.sb(f"C1_xb{k}", [128, 512], BF16, ph) for k in range(2)]
        i = 0
        for kc in range(16):
            for tt in range(4):
                f, g = xt[i % 2], xb[i % 2]
                i += 1
                p.load(f.h[:], io["x2"].ap[kc * 128:(kc + 1) * 128, tt * 512:(tt + 1) * 512],
                       f.b, io["x2"].b)
                p.rop("dve", "tensor_copy", g.h[:], f.h[:], outs=[g.b], reads=[f.b])
                p.store(x2bfD.ap[kc * 128:(kc + 1) * 128, tt * 512:(tt + 1) * 512], g.h[:],
                        g.b, x2bfD.b)
        ctx.phase()
    p.ctx.op("pool", (lambda e, i_=x2bfD.ap.opt(), o_=xAG.ap.opt(): e.collective_compute(
        "AllGather", ALU.bypass, replica_groups=[[0, 1], [2, 3], [4, 5], [6, 7]],
        ins=[i_], outs=[o_])), reads=[x2bfD.b], writes=[xAG.b], cc=True)

    with ExitStack() as ph:
        wq = [p.sb(f"C2_wq{k}", [128, 512], BF16, ph) for k in range(16)]
        wk = [p.sb(f"C2_wk{k}", [128, 128], BF16, ph) for k in range(16)]
        wv = [p.sb(f"C2_wv{k}", [128, 128], BF16, ph) for k in range(16)]
        wtmp = [p.sb(f"C2_wt{k}", [128, 512], F32, ph) for k in range(2)]
        Qh = [p.sb(f"C2_Q{k}", [128, 4096], BF16, ph) for k in range(4)]
        Ku = p.sb("C2_K", [128, 4096], BF16, ph)
        Vu = p.sb("C2_V", [128, 4096], BF16, ph)
        VT = p.sb("C2_VT", [128, 4096], BF16, ph)
        xin = [p.sb(f"C2_xi{k}", [128, 512], BF16, ph) for k in range(2)]
        Et = [p.sb(f"C2_E{k}", [128, 512], BF16, ph) for k in range(2)]
        sm = p.sb("C2_sm", [128, 512], F32, ph)
        rcb = p.sb("C2_rcb", [128, 512], F32, ph)
        rrow = p.sb("C2_rr", [1, 512], F32, ph)
        cxe = [p.sb(f"C2_cx{k}", [128, 512], BF16, ph) for k in range(2)]
        for u in range(2):
            for kc in range(16):
                f = wtmp[kc % 2]
                p.load(f.h[:], io[f"wqu{u}"].ap[kc * 128:(kc + 1) * 128, :], f.b, io[f"wqu{u}"].b)
                p.rop("dve", "tensor_copy", wq[kc].h[:], f.h[:], outs=[wq[kc].b], reads=[f.b])
                p.load(f.h[:, :128], io[f"wku{u}"].ap[kc * 128:(kc + 1) * 128, :], f.b,
                       io[f"wku{u}"].b)
                p.rop("dve", "tensor_copy", wk[kc].h[:], f.h[:, :128], outs=[wk[kc].b],
                      reads=[f.b])
                p.load(f.h[:, :128], io[f"wvu{u}"].ap[kc * 128:(kc + 1) * 128, :], f.b,
                       io[f"wvu{u}"].b)
                p.rop("dve", "tensor_copy", wv[kc].h[:], f.h[:, :128], outs=[wv[kc].b],
                      reads=[f.b])
            for t8 in range(8):
                gts = slice(t8 * 512, (t8 + 1) * 512)
                r = t8 // 4
                lcs = slice((t8 % 4) * 512, (t8 % 4 + 1) * 512)
                for kc in range(16):
                    f = xin[kc % 2]
                    p.load(f.h[:], xAG.ap[r, kc * 128:(kc + 1) * 128, lcs], f.b, xAG.b)
                    for hd in range(4):
                        p.mm(pb[hd].h[:], wq[kc].h[:, hd * 128:(hd + 1) * 128], f.h[:],
                             pb[hd].b, [wq[kc].b, f.b], start=(kc == 0), stop=(kc == 15))
                    p.mm(pb[4].h[:], wk[kc].h[:], f.h[:], pb[4].b, [wk[kc].b, f.b],
                         start=(kc == 0), stop=(kc == 15))
                    p.mm(pb[5].h[:], wv[kc].h[:], f.h[:], pb[5].b, [wv[kc].b, f.b],
                         start=(kc == 0), stop=(kc == 15))
                for hd in range(4):
                    p.rop("dve", "tensor_copy", Qh[hd].h[:, gts], pb[hd].h[:],
                          outs=[Qh[hd].b], reads=[pb[hd].b])
                p.rop("dve", "tensor_copy", Ku.h[:, gts], pb[4].h[:], outs=[Ku.b],
                      reads=[pb[4].b])
                p.rop("dve", "tensor_copy", Vu.h[:, gts], pb[5].h[:], outs=[Vu.b],
                      reads=[pb[5].b])
            for kt in range(32):
                ks = slice(kt * 128, (kt + 1) * 128)
                p.ctx.op("pe", (lambda e, o=pbf.h[:, 0:128], i_=Vu.h[:, ks], id_=idb.h[:]:
                                e.transpose(o, i_, id_)),
                         reads=[Vu.b, idb.b], writes=[pbf.b])
                p.rop("dve", "tensor_copy", VT.h[:, ks], pbf.h[:, 0:128], outs=[VT.b],
                      reads=[pbf.b])
            for hd in range(4):
                for qt in range(8):
                    qs = slice(qt * 512, (qt + 1) * 512)
                    nkt = 4 * qt + 4
                    for kt in range(nkt):
                        ks = slice(kt * 128, (kt + 1) * 128)
                        p.mm(pb[0].h[:], Ku.h[:, ks], Qh[hd].h[:, qs], pb[0].b,
                             [Ku.b, Qh[hd].b], start=True, stop=True)
                        E = Et[kt % 2]
                        off = kt * 128 - qt * 512
                        if off >= 0:
                            p.rop("dve", "tensor_add", sm.h[:], pb[0].h[:],
                                  band.h[:, 384 - off:896 - off], outs=[sm.b],
                                  reads=[pb[0].b, band.b])
                            p.act(E.h[:], sm.h[:], AF.Exp, E.b, [sm.b, c["negC"].b],
                                  scale=ASCALE, bias=c["negC"].h[:])
                        else:
                            p.act(E.h[:], pb[0].h[:], AF.Exp, E.b, [pb[0].b, c["negC"].b],
                                  scale=ASCALE, bias=c["negC"].h[:])
                        p.mm(pb[1].h[0:1, :], c["ones_col_bf"].h[:], E.h[:], pb[1].b,
                             [E.b, c["ones_col_bf"].b], start=(kt == 0), stop=(kt == nkt - 1))
                        p.mm(pb[2].h[:], VT.h[:, ks], E.h[:], pb[2].b, [VT.b, E.b],
                             start=(kt == 0), stop=(kt == nkt - 1))
                    p.rop("dve", "reciprocal", rrow.h[0:1, :], pb[1].h[0:1, :],
                          outs=[rrow.b], reads=[pb[1].b])
                    p.mm(pb[3].h[:], c["ones_row_f"].h[:], rrow.h[0:1, :], pb[3].b,
                         [rrow.b, c["ones_row_f"].b], start=True, stop=True)
                    p.rop("dve", "tensor_copy", rcb.h[:], pb[3].h[:], outs=[rcb.b],
                          reads=[pb[3].b])
                    cx = cxe[qt % 2]
                    p.rop("dve", "tensor_mul", cx.h[:], pb[2].h[:], rcb.h[:],
                          outs=[cx.b], reads=[pb[2].b, rcb.b])
                    p.store(ctxD.ap[u * 512 + hd * 128:u * 512 + (hd + 1) * 128, qs],
                            cx.h[:], cx.b, ctxD.b)
        ctx.phase()

    with ExitStack() as ph:
        wd = [p.sb(f"C3_wd{k}", [128, 2048], BF16, ph) for k in range(8)]
        wtmp = [p.sb(f"C3_wt{k}", [128, 2048], F32, ph) for k in range(2)]
        cx = [p.sb(f"C3_cx{k}", [128, 512], BF16, ph) for k in range(8)]
        ev = [p.sb(f"C3_ev{k}", [128, 512], BF16, ph) for k in range(2)]
        for u in range(2):
            for kc in range(4):
                f = wtmp[kc % 2]
                p.load(f.h[:], io[f"wdu{u}"].ap[kc * 128:(kc + 1) * 128, :], f.b,
                       io[f"wdu{u}"].b)
                p.rop("dve", "tensor_copy", wd[u * 4 + kc].h[:], f.h[:],
                      outs=[wd[u * 4 + kc].b], reads=[f.b])
        for t8 in range(8):
            gts = slice(t8 * 512, (t8 + 1) * 512)
            for kc in range(8):
                p.load(cx[kc].h[:], ctxD.ap[kc * 128:(kc + 1) * 128, gts], cx[kc].b, ctxD.b)
            for oc in range(16):
                bank = pb[oc % 2]
                for kc in range(8):
                    p.mm(bank.h[:], wd[kc].h[:, oc * 128:(oc + 1) * 128], cx[kc].h[:],
                         bank.b, [wd[kc].b, cx[kc].b], start=(kc == 0), stop=(kc == 7))
                o = ev[oc % 2]
                p.rop("act", "copy", o.h[:], bank.h[:], outs=[o.b], reads=[bank.b])
                p.store(outPD.ap[oc * 128:(oc + 1) * 128, gts], o.h[:], o.b, outPD.b)
        ctx.phase()
    p.ctx.op("pool", (lambda e, i_=outPD.ap.opt(), o_=outRD.ap.opt(): e.collective_compute(
        "AllReduce", ALU.add, replica_groups=[[0, 1], [2, 3], [4, 5], [6, 7]],
        ins=[i_], outs=[o_])), reads=[outPD.b], writes=[outRD.b], cc=True)

    with ExitStack() as ph:
        x = [p.sb(f"C5_x{k}", [128, 512], F32, ph) for k in range(16)]
        oa = [p.sb(f"C5_oa{k}", [128, 512], BF16, ph) for k in range(2)]
        ob = [p.sb(f"C5_ob{k}", [128, 512], BF16, ph) for k in range(2)]
        w1 = [p.sb(f"C5_w1{k}", [128, 512], F32, ph) for k in range(2)]
        sqb = [p.sb(f"C5_sq{k}", [128, 512], BF16, ph) for k in range(4)]
        rowt = p.sb("C5_row", [1, 512], F32, ph)
        rowt2 = p.sb("C5_row2", [1, 512], F32, ph)
        ev = [p.sb(f"C5_ev{k}", [128, 512], F32, ph) for k in range(2)]
        for tt in range(4):
            cs = slice(tt * 512, (tt + 1) * 512)
            for oc in range(16):
                rs = slice(oc * 128, (oc + 1) * 128)
                p.load(x[oc].h[:], io["x2"].ap[rs, cs], x[oc].b, io["x2"].b)
                a = oa[oc % 2]
                b_ = ob[oc % 2]
                w = w1[oc % 2]
                p.load(a.h[:], outRD.ap[rs, tt * 512:(tt + 1) * 512], a.b, outRD.b)
                p.load(b_.h[:], outRD.ap[rs, 2048 + tt * 512:2048 + (tt + 1) * 512],
                       b_.b, outRD.b)
                p.rop("dve", "tensor_scalar", w.h[:], a.h[:], sel0.h[:], None, ALU.mult,
                      outs=[w.b], reads=[a.b, sel0.b])
                p.rop("dve", "scalar_tensor_tensor", w.h[:], b_.h[:], sel1.h[:], w.h[:],
                      ALU.mult, ALU.add, outs=[w.b], reads=[b_.b, sel1.b, w.b])
                p.rop("dve", "tensor_add", x[oc].h[:], x[oc].h[:], w.h[:],
                      outs=[x[oc].b], reads=[x[oc].b, w.b])

            def putC(oc, bm, br, cs=cs):
                o = ev[oc % 2]
                p.rop("dve", "tensor_sub", o.h[:], x[oc].h[:], bm.h[:], outs=[o.b],
                      reads=[x[oc].b, bm.b])
                p.rop("dve", "tensor_mul", o.h[:], o.h[:], br.h[:], outs=[o.b],
                      reads=[o.b, br.b])
                p.store(io["x3"].ap[oc * 128:(oc + 1) * 128, cs], o.h[:], o.b, io["x3"].b)
            ln_fm(p, c, x, (pb[0], pb[1], pb[2], pb[3]), sqb, rowt, rowt2, putC)
        ctx.phase()


def build_launch_C(debug=False):
    nc = bass.Bass("TRN2", target_bir_lowering=False, debug=False)
    p = P(nc)
    io = {}

    def inp(nm, shp, dt=F32):
        io[nm] = DT(nc.dram_tensor(nm, list(shp), dt, kind="ExternalInput").ap(), nm)
    inp("x2", [D, TCORE])
    for u in range(2):
        inp(f"wqu{u}", [2048, 512]); inp(f"wku{u}", [2048, 128])
        inp(f"wvu{u}", [2048, 128]); inp(f"wdu{u}", [512, 2048])
    inp("Band", [128, 896]); inp("IDB", [128, 128], BF16)
    inp("sel0", [128, 1]); inp("sel1", [128, 1])
    io["x3"] = DT(nc.dram_tensor("x3", [D, TCORE], F32, kind="ExternalOutput").ap(), "x3")
    with p.es:
        c = _mk_consts(p)
        emit_l1_attn(p, c, io, debug=debug)
        p.ctx.finish()
        p.ctx.build(p.es)
    return nc


def prep_inputs_C(ins, x2):
    f32 = np.float32
    wq = np.asarray(ins["L1_attn_w_q"], f32)
    wkv = np.asarray(ins["L1_attn_w_kv"], f32)
    wd = np.asarray(ins["L1_attn_w_dense"], f32)
    band = np.zeros((128, 896), f32)
    for kk in range(128):
        for cc in range(896):
            if cc - kk < 384:
                band[kk, cc] = -1e30
    xs = _slices_fm(x2)
    maps = []
    for cid in range(NCORE):
        b, h = cid // 2, cid % 2
        m = {
            "x2": xs[cid],
            "Band": band,
            "IDB": np.eye(128, dtype=f32).astype(BF),
            "sel0": np.full((128, 1), 1.0 - h, f32),
            "sel1": np.full((128, 1), float(h), f32),
        }
        for u in range(2):
            g = 2 * h + u
            m[f"wqu{u}"] = np.ascontiguousarray(wq[:, g * 512:(g + 1) * 512])
            m[f"wku{u}"] = np.ascontiguousarray(wkv[:, g * 128:(g + 1) * 128])
            m[f"wvu{u}"] = np.ascontiguousarray(wkv[:, 512 + g * 128:512 + (g + 1) * 128])
            m[f"wdu{u}"] = np.ascontiguousarray(wd[g * 512:(g + 1) * 512, :])
        maps.append(m)
    return maps


def _dequant_decode(p, c, cf, vout, tmp):
    s, m4, r12 = tmp
    p.act(s.h[:], cf.h[:], AF.Sign, s.b, [cf.b, c["c75"].b], scale=-1.0, bias=c["c75"].h[:])
    p.rop("dve", "scalar_tensor_tensor", m4.h[:], s.h[:], 4.0, cf.h[:], ALU.mult, ALU.add,
          outs=[m4.b], reads=[s.b, cf.b])
    p.act(r12.h[:], m4.h[:], AF.Relu, r12.b, [m4.b, c["cm4"].b], scale=0.5, bias=c["cm4"].h[:])
    p.rop("dve", "scalar_tensor_tensor", r12.h[:], m4.h[:], 0.5, r12.h[:], ALU.mult, ALU.add,
          outs=[r12.b], reads=[m4.b, r12.b])
    p.act(vout.h[:], m4.h[:], AF.Relu, vout.b, [m4.b, c["cm10"].b], bias=c["cm10"].h[:])
    p.rop("dve", "tensor_add", r12.h[:], r12.h[:], vout.h[:], outs=[r12.b], reads=[r12.b, vout.b])
    p.rop("dve", "tensor_scalar_add", r12.h[:], r12.h[:], -2.0, outs=[r12.b], reads=[r12.b])
    p.rop("dve", "tensor_mul", vout.h[:], s.h[:], r12.h[:], outs=[vout.b], reads=[s.b, r12.b])


def emit_l1_ffn_dequant(p, c, io, debug=False):
    ctx = p.ctx
    pb = [p.ps(f"D0_pb{i}", [128, 512]) for i in range(2)]

    gshD = p.dram("D_gsh", [2048, 2048], BF16)
    ushD = p.dram("D_ush", [2048, 2048], BF16)
    dshD = p.dram("D_dsh", [16384, 256], BF16)
    io["gAG"] = p.dram("D_gAG", [8, 2048, 2048], BF16)
    io["uAG"] = p.dram("D_uAG", [8, 2048, 2048], BF16)
    io["dAG"] = p.dram("D_dAG", [8, 16384, 256], BF16)

    sel64 = p.sb("D_sel", [64, 2048])
    idf = p.sb("D0_idf", [128, 128])
    idb = p.sb("D0_idb", [128, 128], BF16)
    for t, key in [(sel64, "sel64"), (idf, "idf"), (idb, "IDB")]:
        p.load(t.h[:], io[key].ap[:], t.b, io[key].b)

    with ExitStack() as ph:
        pbf = T(ph.enter_context(p.nc.psum_tensor("D0_pbf", [128, 1024], BF16)), "D0_pbf")
        pk = p.sb("D0_pk", [128, 1024], I32, ph)
        ii = p.sb("D0_ii", [128, 1024], I32, ph)
        cf = [p.sb(f"D0_cf{k}", [128, 1024], F32, ph) for k in range(2)]
        vv = [p.sb(f"D0_v{k}", [128, 1024], F32, ph) for k in range(2)]
        tmp = [p.sb(f"D0_t{k}", [128, 1024], F32, ph) for k in range(3)]
        wrow = p.sb("D0_wr", [128, 2048], BF16, ph)
        tb = p.sb("D0_tb", [128, 128], BF16, ph)
        wT = [p.sb(f"D0_wT{k}", [128, 2048], BF16, ph) for k in range(16)]
        ssc = p.sb("D0_ssc", [128, 64], F32, ph)
        sT = p.sb("D0_sT", [64, 2048], F32, ph)
        dscT = p.sb("D0_dT", [64, 256], F32, ph)
        sout = p.sb("D0_so", [128, 512], BF16, ph)

        def dq_tile(pksrc, r0, wcol, nwords):
            p.load(pk.h[:, :nwords], pksrc.ap[r0:r0 + 128, wcol:wcol + nwords], pk.b, pksrc.b)
            p.rop("dve", "tensor_single_scalar", ii.h[:, :nwords], pk.h[:, :nwords], 15,
                  ALU.bitwise_and, outs=[ii.b], reads=[pk.b])
            p.rop("dve", "tensor_copy", cf[0].h[:, :nwords], ii.h[:, :nwords],
                  outs=[cf[0].b], reads=[ii.b])
            p.rop("dve", "tensor_single_scalar", ii.h[:, :nwords], pk.h[:, :nwords], 4,
                  ALU.logical_shift_right, outs=[ii.b], reads=[pk.b])
            p.rop("dve", "tensor_copy", cf[1].h[:, :nwords], ii.h[:, :nwords],
                  outs=[cf[1].b], reads=[ii.b])
            _dequant_decode(p, c, cf[0], vv[0], tmp)
            _dequant_decode(p, c, cf[1], vv[1], tmp)
            ap_even = bass.AP(wrow.h, 0, [[2048, 128], [2, nwords]])
            ap_odd = bass.AP(wrow.h, 1, [[2048, 128], [2, nwords]])
            p.rop("dve", "tensor_copy", ap_even, vv[0].h[:, :nwords], outs=[wrow.b],
                  reads=[vv[0].b])
            p.rop("dve", "tensor_copy", ap_odd, vv[1].h[:, :nwords], outs=[wrow.b],
                  reads=[vv[1].b])

        for wname, pkk, sck, shD in [("g", "gpk", "gsc", gshD), ("u", "upk", "usc", ushD)]:
            for nb in range(16):
                dq_tile(io[pkk], nb * 128, 0, 1024)
                for kc in range(16):
                    p.ctx.op("pe", (lambda e, o=pbf.h[:, 0:128], i_=wrow.h[:, kc * 128:(kc + 1) * 128],
                                    id_=idb.h[:]: e.transpose(o, i_, id_)),
                             reads=[wrow.b, idb.b], writes=[pbf.b])
                    p.rop("dve", "tensor_copy", wT[kc].h[:, nb * 128:(nb + 1) * 128],
                          pbf.h[:, 0:128], outs=[wT[kc].b], reads=[pbf.b])
            for nb in range(16):
                p.load(ssc.h[:], io[sck].ap[nb * 128:(nb + 1) * 128, :], ssc.b, io[sck].b)
                p.ctx.op("pe", (lambda e, o=pb[0].h[0:64, 0:128], i_=ssc.h[:], id_=idf.h[:]:
                                e.transpose(o, i_, id_)), reads=[ssc.b, idf.b], writes=[pb[0].b])
                p.rop("dve", "tensor_copy", sT.h[:, nb * 128:(nb + 1) * 128], pb[0].h[0:64, 0:128],
                      outs=[sT.b], reads=[pb[0].b])
            for kc in range(16):
                for q4 in range(4):
                    fs = slice(q4 * 512, (q4 + 1) * 512)
                    p.mm(pb[1].h[:], sel64.h[:, kc * 128:(kc + 1) * 128], sT.h[:, fs],
                         pb[1].b, [sel64.b, sT.b], start=True, stop=True)
                    p.rop("dve", "tensor_mul", sout.h[:], wT[kc].h[:, fs], pb[1].h[:],
                          outs=[sout.b], reads=[wT[kc].b, pb[1].b])
                    p.store(shD.ap[kc * 128:(kc + 1) * 128, fs], sout.h[:], sout.b, shD.b)
        for ht in range(2):
            for wc in range(8):
                p.load(ssc.h[:, :64], io["dsc"].ap[ht * 128:(ht + 1) * 128,
                                                   wc * 64:(wc + 1) * 64], ssc.b, io["dsc"].b)
                p.ctx.op("pe", (lambda e, o=pb[0].h[0:64, 0:128], i_=ssc.h[:, 0:64],
                                id_=idf.h[:]: e.transpose(o, i_, id_)),
                         reads=[ssc.b, idf.b], writes=[pb[0].b])
                p.rop("dve", "tensor_copy", dscT.h[:, 0:128], pb[0].h[0:64, 0:128],
                      outs=[dscT.b], reads=[pb[0].b])
                dq_tile(io["dpk"], ht * 128, wc * 1024, 1024)
                for j in range(16):
                    p.ctx.op("pe", (lambda e, o=pbf.h[:, 0:128],
                                    i_=wrow.h[:, j * 128:(j + 1) * 128],
                                    id_=idb.h[:]: e.transpose(o, i_, id_)),
                             reads=[wrow.b, idb.b], writes=[pbf.b])
                    p.rop("dve", "tensor_copy", tb.h[:], pbf.h[:, 0:128], outs=[tb.b],
                          reads=[pbf.b])
                    p.mm(pb[1].h[:, 0:128], sel64.h[:, j * 128:(j + 1) * 128],
                         dscT.h[:, 0:128], pb[1].b, [sel64.b, dscT.b], start=True, stop=True)
                    p.rop("dve", "tensor_mul", sout.h[:, 0:128], tb.h[:], pb[1].h[:, 0:128],
                          outs=[sout.b], reads=[tb.b, pb[1].b])
                    p.store(dshD.ap[wc * 2048 + j * 128:wc * 2048 + (j + 1) * 128,
                                    ht * 128:(ht + 1) * 128],
                            sout.h[:, 0:128], sout.b, dshD.b)
        ctx.phase(exclude=("cc",))
    for src_, dst_ in [(gshD, io["gAG"]), (ushD, io["uAG"]), (dshD, io["dAG"])]:
        p.ctx.op("pool", (lambda e, i_=src_.ap.opt(), o_=dst_.ap.opt(): e.collective_compute(
            "AllGather", ALU.bypass, replica_groups=[[0, 1, 2, 3, 4, 5, 6, 7]],
            ins=[i_], outs=[o_])), reads=[src_.b], writes=[dst_.b], cc=True)


def emit_l1_ffn_main(p, c, io, debug=False):
    ctx = p.ctx
    gAG, uAG, dAG = io["gAG"], io["uAG"], io["dAG"]
    pb = [p.ps(f"D_pb{i}", [128, 512]) for i in range(8)]
    wdT = p.dram("D_wdT", [16384, 2048], BF16)
    with ExitStack() as ph:
        rp = [p.sb(f"Drp{k}", [128, 2048], BF16, ph) for k in range(4)]
        for ncg in range(128):
            f = rp[ncg % 4]
            for r_ in range(8):
                p.load(f.h[:, r_ * 256:(r_ + 1) * 256],
                       dAG.ap[r_, ncg * 128:(ncg + 1) * 128, 0:256], f.b, dAG.b)
            p.store(wdT.ap[ncg * 128:(ncg + 1) * 128, :], f.h[:], f.b, wdT.b)
        ctx.phase(exclude=("cc",))
    with ExitStack() as ph:
        x = [p.sb(f"D1_x{k}", [128, 512], F32, ph) for k in range(16)]
        xbf = [p.sb(f"D1_xb{k}", [128, 512], BF16, ph) for k in range(16)]
        gate = [p.sb(f"D1_g{k}", [128, 512], BF16, ph) for k in range(16)]
        inter = [p.sb(f"D1_i{k}", [128, 512], BF16, ph) for k in range(16)]
        acc = [p.sb(f"D1_a{k}", [128, 512], F32, ph) for k in range(16)]
        wpool = [p.sb(f"D1_wp{k}", [128, 1024], BF16, ph) for k in range(16)]
        gtmp = [p.sb(f"D1_gt{k}", [128, 512], F32, ph) for k in range(2)]
        sqb = [p.sb(f"D1_sq{k}", [128, 512], BF16, ph) for k in range(4)]
        rowt = p.sb("D1_row", [1, 512], F32, ph)
        rowt2 = p.sb("D1_row2", [1, 512], F32, ph)
        ev = [p.sb(f"D1_ev{k}", [128, 512], F32, ph) for k in range(2)]
        wi_ = [0]

        def wnext():
            wi_[0] += 1
            return wpool[wi_[0] % 16]
        for tt in range(4):
            cs = slice(tt * 512, (tt + 1) * 512)
            for kc in range(16):
                p.load(x[kc].h[:], io["x3"].ap[kc * 128:(kc + 1) * 128, cs], x[kc].b,
                       io["x3"].b)
                p.rop("dve", "tensor_copy", xbf[kc].h[:], x[kc].h[:], outs=[xbf[kc].b],
                      reads=[x[kc].b])
            for g in range(8):
                for AGW, is_up in [(gAG, False), (uAG, True)]:
                    for sg in range(2):
                        for kc in range(16):
                            f = wnext()
                            p.load(f.h[:], AGW.ap[g, kc * 128:(kc + 1) * 128,
                                                  sg * 1024:(sg + 1) * 1024], f.b, AGW.b)
                            for j in range(8):
                                p.mm(pb[j].h[:], f.h[:, j * 128:(j + 1) * 128], xbf[kc].h[:],
                                     pb[j].b, [f.b, xbf[kc].b], start=(kc == 0), stop=(kc == 15))
                        for j in range(8):
                            nc_ = sg * 8 + j
                            if not is_up:
                                p.rop("act", "copy", gate[nc_].h[:], pb[j].h[:],
                                      outs=[gate[nc_].b], reads=[pb[j].b])
                            else:
                                ga = gtmp[j % 2]
                                p.act(ga.h[:], gate[nc_].h[:], AF.Gelu_apprx_tanh, ga.b,
                                      [gate[nc_].b])
                                p.rop("dve", "tensor_mul", inter[nc_].h[:], ga.h[:], pb[j].h[:],
                                      outs=[inter[nc_].b], reads=[ga.b, pb[j].b])
                for half in range(2):
                    for nc_ in range(16):
                        f = wnext()
                        p.load(f.h[:], wdT.ap[g * 2048 + nc_ * 128:g * 2048 + (nc_ + 1) * 128,
                                              half * 1024:(half + 1) * 1024], f.b, wdT.b)
                        for j in range(8):
                            p.mm(pb[j].h[:], f.h[:, j * 128:(j + 1) * 128], inter[nc_].h[:],
                                 pb[j].b, [f.b, inter[nc_].b], start=(nc_ == 0), stop=(nc_ == 15))
                    for j in range(8):
                        oc = half * 8 + j
                        if g == 0:
                            p.rop("dve", "tensor_copy", acc[oc].h[:], pb[j].h[:],
                                  outs=[acc[oc].b], reads=[pb[j].b])
                        else:
                            p.rop("dve", "tensor_add", acc[oc].h[:], acc[oc].h[:], pb[j].h[:],
                                  outs=[acc[oc].b], reads=[acc[oc].b, pb[j].b])
            for oc in range(16):
                p.rop("dve", "tensor_add", x[oc].h[:], x[oc].h[:], acc[oc].h[:],
                      outs=[x[oc].b], reads=[x[oc].b, acc[oc].b])

            def putD(oc, bm, br, cs=cs):
                o = ev[oc % 2]
                p.rop("dve", "tensor_sub", o.h[:], x[oc].h[:], bm.h[:], outs=[o.b],
                      reads=[x[oc].b, bm.b])
                p.rop("dve", "tensor_mul", o.h[:], o.h[:], br.h[:], outs=[o.b],
                      reads=[o.b, br.b])
                p.store(io["out"].ap[oc * 128:(oc + 1) * 128, cs], o.h[:], o.b, io["out"].b)
            ln_fm(p, c, x, (pb[0], pb[1], pb[2], pb[3]), sqb, rowt, rowt2, putD)
            ctx.phase()


def emit_l1_ffn(p, c, io, debug=False):
    emit_l1_ffn_dequant(p, c, io, debug)
    emit_l1_ffn_main(p, c, io, debug)


def build_launch_D(debug=False):
    nc = bass.Bass("TRN2", target_bir_lowering=False, debug=False)
    p = P(nc)
    io = {}

    def inp(nm, shp, dt=F32):
        io[nm] = DT(nc.dram_tensor(nm, list(shp), dt, kind="ExternalInput").ap(), nm)
    inp("x3", [D, TCORE])
    inp("gpk", [2048, 1024], I32); inp("gsc", [2048, 64])
    inp("upk", [2048, 1024], I32); inp("usc", [2048, 64])
    inp("dpk", [256, 8192], I32); inp("dsc", [256, 512])
    inp("sel64", [64, 2048]); inp("idf", [128, 128]); inp("IDB", [128, 128], BF16)
    io["out"] = DT(nc.dram_tensor("out", [D, TCORE], F32, kind="ExternalOutput").ap(), "out")
    with p.es:
        c = _mk_consts(p)
        emit_l1_ffn(p, c, io, debug=debug)
        p.ctx.finish()
        p.ctx.build(p.es)
    return nc


def prep_inputs_D(ins, x3):
    f32 = np.float32
    sel64 = np.zeros((64, 2048), f32)
    for kc in range(16):
        for pp in range(128):
            sel64[kc * 4 + pp // 32, kc * 128 + pp] = 1.0
    com = {
        "sel64": sel64,
        "idf": np.eye(128, dtype=f32),
        "IDB": np.eye(128, dtype=f32).astype(BF),
    }
    gp = np.asarray(ins["L1_ffn_gate_W_packed"], np.int32)
    gs = np.asarray(ins["L1_ffn_gate_scales"], f32)
    up = np.asarray(ins["L1_ffn_up_W_packed"], np.int32)
    us = np.asarray(ins["L1_ffn_up_scales"], f32)
    dp = np.asarray(ins["L1_ffn_down_W_packed"], np.int32)
    ds = np.asarray(ins["L1_ffn_down_scales"], f32)
    xs = _slices_fm(x3)
    maps = []
    for cid in range(NCORE):
        b, h = cid // 2, cid % 2
        m = dict(com)
        m["x3"] = xs[cid]
        m["gpk"] = np.ascontiguousarray(gp[cid * 2048:(cid + 1) * 2048])
        m["gsc"] = np.ascontiguousarray(gs[cid * 2048:(cid + 1) * 2048])
        m["upk"] = np.ascontiguousarray(up[cid * 2048:(cid + 1) * 2048])
        m["usc"] = np.ascontiguousarray(us[cid * 2048:(cid + 1) * 2048])
        m["dpk"] = np.ascontiguousarray(dp[cid * 256:(cid + 1) * 256])
        m["dsc"] = np.ascontiguousarray(ds[cid * 256:(cid + 1) * 256])
        maps.append(m)
    return maps


def build_fused(debug=False):
    nc = bass.Bass("TRN2", target_bir_lowering=False, debug=False)
    p = P(nc)
    io = {}

    def inp(nm, shp, dt=F32):
        io[nm] = DT(nc.dram_tensor(nm, list(shp), dt, kind="ExternalInput").ap(), nm)
    inp("x", [D, TCORE])
    for nm, shp in [("wa", [D, D]), ("wi", [D, D]), ("wv", [D, D]), ("wo", [D, D]),
                    ("wgu", [D, 16384]), ("wdn", [8192, D])]:
        inp(nm, shp)
    for nm in ["prew", "dcb", "ffnw"]:
        inp(nm, [128, 16])
    inp("wsel", [128, 1])
    inp("dw", [2048, 1024]); inp("uw", [1024, 2048]); inp("rw", [2048, 8])
    inp("s1", [1024, 1024]); inp("s2", [1024, 1024]); inp("cw", [2048, 2048])
    for i, hh in enumerate(HID):
        inp(f"e{i}w1", [1024, hh]); inp(f"e{i}w2", [hh, 1024])
        inp(f"e{i}b1f", [128, hh // 128]); inp(f"e{i}b2r", [1, 1024])
    inp("dbf", [128, 8]); inp("sb1f", [128, 8]); inp("sb2r", [1, 1024])
    inp("ubr", [1, 2048]); inp("cbr", [1, 2048]); inp("rbr", [1, 8])
    inp("negc", [1, 8]); inp("tbeps", [128, 8]); inp("idf", [128, 128])
    for u in range(2):
        inp(f"wqu{u}", [2048, 512]); inp(f"wku{u}", [2048, 128])
        inp(f"wvu{u}", [2048, 128]); inp(f"wdu{u}", [512, 2048])
    inp("Band", [128, 896]); inp("IDB", [128, 128], BF16)
    inp("sel0", [128, 1]); inp("sel1", [128, 1])
    inp("gpk", [2048, 1024], I32); inp("gsc", [2048, 64])
    inp("upk", [2048, 1024], I32); inp("usc", [2048, 64])
    inp("dpk", [256, 8192], I32); inp("dsc", [256, 512])
    inp("sel64", [64, 2048])
    io["x1"] = p.dram("x1", [D, TCORE])
    io["x2"] = p.dram("x2", [D, TCORE])
    io["x3"] = p.dram("x3", [D, TCORE])
    io["out"] = DT(nc.dram_tensor("out", [D, TCORE], F32, kind="ExternalOutput").ap(), "out")
    with p.es:
        c = _mk_consts(p)
        for tag, emitter in [("blkD0", emit_l1_ffn_dequant), ("blkA", emit_l0_attn),
                             ("blkB", emit_l0_ffn), ("blkC", emit_l1_attn),
                             ("blkD", emit_l1_ffn_main)]:
            p.ctx.tag = tag
            with p.scope():
                emitter(p, c, io, debug=False)
        p.ctx.tag = None
        p.ctx.finish()
        p.ctx.build(p.es)
    return nc


def prep_fused(ins):
    mA = prep_inputs_A(ins)
    mB = prep_inputs_B(ins, [None] * NCORE)
    mC = prep_inputs_C(ins, [None] * NCORE)
    mD = prep_inputs_D(ins, [None] * NCORE)
    maps = []
    for cid in range(NCORE):
        m = {}
        for mm in (mA[cid], mB[cid], mC[cid], mD[cid]):
            for k, v in mm.items():
                if v is not None:
                    m[k] = v
        maps.append(m)
    return maps


def kernel(**inputs):
    cores = list(range(NCORE))
    res = run_bass_kernel_spmd(_nc_for("F", build_fused), prep_fused(inputs), core_ids=cores)
    out = np.empty((4, 4096, 2048), np.float32)
    for cid in cores:
        b, h = cid // 2, cid % 2
        out[b, h * 2048:(h + 1) * 2048, :] = res.results[cid]["out"].T
    return out
